# version 03 (final)
# baseline (speedup 1.0000x reference)
import numpy as np
import math
from contextlib import ExitStack

import ml_dtypes
import concourse.bass as bass
import concourse.mybir as mybir
from concourse.bass import IndirectOffsetOnAxis
from concourse.bass_utils import run_bass_kernel_spmd

F32 = mybir.dt.float32
BF16 = mybir.dt.bfloat16
U32 = mybir.dt.uint32
AF = mybir.ActivationFunctionType
ALU = mybir.AluOpType
AX = mybir.AxisListType

T = 4096
D = 1024
NCHUNK = D // 128
NTILE = T // 128
NCORE = 8

MAGIC = 12582912.0


class Prog:

    ENGINES = ("tensor", "vector", "scalar", "gpsimd", "sync")

    def __init__(self, nc, cm):
        self.nc = nc
        self.cm = cm
        self.ops = {e: [] for e in self.ENGINES}
        self.sems = {}
        self.phase_idx = 0

    def sem(self, name):
        if name not in self.sems:
            h = self.cm.enter_context(self.nc.semaphore(name))
            self.sems[name] = [h, 0]
        return self.sems[name]

    def op(self, eng, fn, mark=None, inc=1):
        if mark is None:
            self.ops[eng].append(fn)
            return None
        s = self.sem(mark)
        s[1] += inc
        h, tgt = s[0], s[1]
        self.ops[eng].append(lambda e, fn=fn, h=h, inc=inc: fn(e).then_inc(h, inc))
        return (mark, tgt)

    def dma(self, queue, out, in_, **kw):
        sname = "dma_" + queue
        return self.op(queue, lambda e, o=out, i=in_, kw=kw: e.dma_start(out=o, in_=i, **kw),
                       mark=sname, inc=16)

    def tok(self, sname):
        return (sname, self.sems[sname][1]) if sname in self.sems else None

    def wait(self, eng, token):
        if token is None:
            return
        name, val = token
        h = self.sem(name)[0]
        self.ops[eng].append(lambda e, h=h, v=val: e.wait_ge(h, v))

    def wait_many(self, eng, tokens):
        for t in tokens:
            self.wait(eng, t)

    def barrier(self, name=None):
        self.phase_idx += 1
        import os as _os
        if not bool(int(_os.environ.get("K_OLD_BARRIER", "0"))):
            dma_toks = [self.tok(s) for s in ("dma_sync", "dma_gpsimd", "dma_scalar",
                                              "dma_vector", "dma_tensor", "cc") if s in self.sems]
            for e in self.ENGINES:
                for t in dma_toks:
                    self.wait(e, t)
        s = self.sem("phase")
        for e in self.ENGINES:
            s[1] += 1
            h = s[0]
            self.ops[e].append(lambda eng, h=h: eng.sem_inc(h, 1))
        tgt = s[1]
        for e in self.ENGINES:
            h = s[0]
            self.ops[e].append(lambda eng, h=h, v=tgt: eng.wait_ge(h, v))

    def build(self):
        nc = self.nc
        with nc.Block() as block:
            block.tensor(lambda e: [f(e) for f in self.ops["tensor"]])
            block.vector(lambda e: [f(e) for f in self.ops["vector"]])
            block.scalar(lambda e: [f(e) for f in self.ops["scalar"]])
            block.gpsimd(lambda e: [f(e) for f in self.ops["gpsimd"]])
            block.sync(lambda e: [f(e) for f in self.ops["sync"]])
        return nc




def sbuf(prog, name, shape, dtype):
    return prog.cm.enter_context(prog.nc.sbuf_tensor(name, shape, dtype))


def psum(prog, name, shape, dtype=F32):
    return prog.cm.enter_context(prog.nc.psum_tensor(name, shape, dtype))


def scoped(prog):
    return ExitStack()


def ln_stats_tm(prog, p, xt, stat, eps=1e-5, n=D):
    nm, rs = stat["negmean"], stat["rstd"]
    sm, sq = stat["sum"], stat["sq"]
    t1 = prog.op("vector", lambda e: e.reduce_sum(sm[:, :], xt[:, :], axis=AX.X), mark="dve")
    ta = prog.op("scalar", lambda e: e.activation(out=stat["sqscr"][:, :], in_=xt[:, :], func=AF.Square,
                                                  accum_out=sq[:, :]), mark="act")
    prog.wait("vector", t1)
    t2 = prog.op("vector", lambda e: e.tensor_scalar(nm[:, :], sm[:, :], float(np.float32(-1.0 / n)), None,
                                                     op0=ALU.mult), mark="dve")
    prog.wait("vector", t2)
    t3 = prog.op("vector", lambda e: e.tensor_tensor(stat["m2"][:, :], nm[:, :], nm[:, :], op=ALU.mult),
                 mark="dve")
    prog.wait("vector", t3)
    prog.wait("vector", ta)
    t4 = prog.op("vector", lambda e: e.tensor_scalar(stat["var"][:, :], sq[:, :], float(np.float32(1.0 / n)),
                                                     stat["m2"][:, :], op0=ALU.mult, op1=ALU.subtract),
                 mark="dve")
    prog.wait("vector", t4)
    t5 = prog.op("vector", lambda e: e.tensor_scalar(stat["var"][:, :], stat["var"][:, :], eps, None,
                                                     op0=ALU.add), mark="dve")
    prog.wait("scalar", t5)
    t6 = prog.op("scalar", lambda e: e.activation(out=stat["std"][:, :], in_=stat["var"][:, :], func=AF.Sqrt),
                 mark="act")
    prog.wait("vector", t6)
    t7 = prog.op("vector", lambda e: e.reciprocal(rs[:, :], stat["std"][:, :]), mark="dve")
    prog.wait("vector", t7)
    return t7


def mk_stats(prog, name):
    s = {}
    for k in ("negmean", "rstd", "sum", "sq", "m2", "var", "std"):
        s[k] = sbuf(prog, f"{name}_{k}", [128, 1], F32)
    s["sqscr"] = sbuf(prog, f"{name}_sqscr", [128, D], BF16)
    return s


def ceil_div(a, b):
    return (a + b - 1) // b


def pe_transpose_tile(prog, src_ap, dst_ap, ps_bank, ident, wait_toks=(), dtype_bf16=True,
                      nrow=128, ncol=128):
    for t in wait_toks:
        prog.wait("tensor", t)
    tp = prog.op("tensor", lambda e, s=src_ap, p=ps_bank, i=ident: e.transpose(p[0:ncol, 0:nrow], s, i),
                 mark="pe")
    prog.wait("vector", tp)
    return prog.op("vector", lambda e, p=ps_bank, d=dst_ap: e.tensor_copy(d, p[0:ncol, 0:nrow]),
                   mark="dve")



def proj_fm(prog, name, rhs_off_fn, Wsrc, dst_write_fn, Din, Dout, tokn,
            rhs_tok_fn=None, wq_engine="gpsimd", ps_banks=None, fstart=0, tbn=512,
            wbufs=None, wbuf_state=None):
    ntb = ceil_div(tokn, tbn)
    ndc = Din // 128
    nfc = Dout // 128
    if ps_banks is None:
        ps_banks = [psum(prog, f"{name}_ps{i}", [128, tbn]) for i in range(2)]
    nps = len(ps_banks)
    evac_toks = {}
    all_toks = []
    from_dram = Wsrc[0] == "dram"
    if from_dram:
        Wap = Wsrc[1]
        if wbufs is None:
            wbufs = [sbuf(prog, f"{name}_w{i}", [128, ndc * 128], BF16) for i in range(2)]
        wtoks = wbuf_state if wbuf_state is not None else [None, None]

        def load_w(fc):
            b = fc % 2
            f0 = (fstart + fc) * 128
            src = Wap[0:Din, f0:f0 + 128].rearrange("(c p) f -> p c f", p=128)
            dstw = wbufs[b][:, :].rearrange("p (c f) -> p c f", f=128)
            wtoks[b] = prog.dma(wq_engine, dstw, src)
        load_w(0)
        if nfc > 1:
            load_w(1)
    for fc in range(nfc):
        b = fc % 2
        if from_dram:
            prog.wait("tensor", wtoks[b])
        for tb in range(ntb):
            tbn_i = min(tbn, tokn - tb * tbn)
            bank = (fc * ntb + tb) % nps
            pb = ps_banks[bank]
            prev = evac_toks.get(bank)
            if prev is not None:
                prog.wait("tensor", prev)
            if rhs_tok_fn is not None:
                prog.wait("tensor", rhs_tok_fn(tb))
            mm_tok = None
            for dc in range(ndc):
                if from_dram:
                    lhsT = wbufs[b][:, dc * 128:(dc + 1) * 128]
                else:
                    _, buf, stride = Wsrc
                    o = dc * stride + (fstart + fc) * 128
                    lhsT = buf[:, o:o + 128]
                rhs = rhs_off_fn(dc, tb, tbn_i)
                mm_tok = prog.op("tensor",
                                 lambda e, p=pb, l=lhsT, r=rhs, s=(dc == 0), t=(dc == ndc - 1), n=tbn_i:
                                 e.matmul(p[:, 0:n], l, r, start=s, stop=t),
                                 mark="pe" if dc == ndc - 1 else None)
            tok = dst_write_fn(fc, tb, pb, mm_tok, tbn_i)
            evac_toks[bank] = tok
            all_toks.append(tok)
        if from_dram and fc + 2 < nfc:
            prog.wait(wq_engine, prog.tok("pe"))
            load_w(fc + 2)
    return all_toks


def act_evac_writer(prog, dst_fm, tokn, bias_sb=None, act=AF.Copy, scale=None, mark="act",
                    eng="scalar"):
    if act == AF.Copy and bias_sb is not None:
        act = AF.Identity
    """dst_write_fn writing act(psum*scale+bias) into an FM buffer laid out
    dst_fm[:, fc*tokn + tb*512 ...]."""
    def w(fc, tb, pb, mm_tok, tbn_i):
        prog.wait(eng, mm_tok)
        kw = {}
        if bias_sb is not None:
            kw["bias"] = bias_sb[:, fc:fc + 1]
        if scale is not None:
            kw["scale"] = scale
        dst = dst_fm[:, fc * tokn + tb * 512: fc * tokn + tb * 512 + tbn_i]
        return prog.op(eng, lambda e, d=dst, p=pb, n=tbn_i, kw=kw:
                       e.activation(out=d, in_=p[:, 0:n], func=act, **kw), mark=mark)
    return w



def emit_quantize(prog, name, Wap, nrows, ncols, dstT_sb, dst_stride, ident_bf, eps=1e-5):
    ntiles = nrows // 128
    ncc = ncols // 128
    wt = [sbuf(prog, f"{name}_qw{i}", [128, ncols], F32) for i in range(2)]
    qb = sbuf(prog, f"{name}_qb", [128, ncols], BF16)
    gpart = sbuf(prog, f"{name}_gp", [128, ntiles], F32)
    red = sbuf(prog, f"{name}_red", [128, 1], F32)
    redT = sbuf(prog, f"{name}_redT", [1, 128], F32)
    tot = sbuf(prog, f"{name}_tot", [1, 1], F32)
    rqb = sbuf(prog, f"{name}_rqb", [128, 1], F32)
    ps_t = psum(prog, f"{name}_pst", [128, 128], F32)
    ps_tb = psum(prog, f"{name}_pstb", [128, 128], BF16)
    dtoks = [None, None]
    rtoks = [None, None]
    dtoks[0] = prog.dma("gpsimd", wt[0][:, :], Wap[0:128, :])
    for i in range(ntiles):
        b = i % 2
        if i + 1 < ntiles:
            if rtoks[(i + 1) % 2] is not None:
                prog.wait("gpsimd", rtoks[(i + 1) % 2])
            dtoks[(i + 1) % 2] = prog.dma("gpsimd", wt[(i + 1) % 2][:, :],
                                          Wap[(i + 1) * 128:(i + 2) * 128, :])
        prog.wait("vector", dtoks[b])
        rtoks[b] = prog.op("vector", lambda e, i=i, b=b: e.reduce_sum(
            gpart[:, i:i + 1], wt[b][:, :], axis=AX.X, apply_absolute_value=True), mark="dve")
    for tkn in rtoks:
        prog.wait("vector", tkn)
    t0 = prog.op("vector", lambda e: e.reduce_sum(red[:, :], gpart[:, :], axis=AX.X), mark="dve")
    prog.wait("vector", t0)
    prog.wait("tensor", t0)
    t1 = prog.op("tensor", lambda e: e.transpose(ps_t[0:1, 0:128], red[:, 0:1], prog.idf32[:, :]), mark="pe")
    prog.wait("vector", t1)
    tc = prog.op("vector", lambda e: e.tensor_copy(redT[:, :], ps_t[0:1, 0:128]), mark="dve")
    prog.wait("vector", tc)
    tr = prog.op("vector", lambda e: e.reduce_sum(tot[:, :], redT[:, :], axis=AX.X), mark="dve")
    prog.wait("vector", tr)
    tg = prog.op("vector", lambda e: e.tensor_scalar(tot[:, :], tot[:, :], float(np.float32(1.0 / (nrows * ncols))),
                                                     float(np.float32(eps)), op0=ALU.mult, op1=ALU.add), mark="dve")
    prog.wait("vector", tg)
    t2 = prog.op("vector", lambda e: e.reciprocal(tot[:, :], tot[:, :]), mark="dve")
    prog.wait("tensor", t2)
    t3 = prog.op("tensor", lambda e: e.matmul(ps_t[:, 0:1], prog.ones_f32[:, :], tot[:, :],
                                              start=True, stop=True), mark="pe")
    prog.wait("vector", t3)
    t4 = prog.op("vector", lambda e: e.tensor_copy(rqb[:, :], ps_t[:, 0:1]), mark="dve")
    prog.wait("vector", t4)
    dtoks = [None, None]
    qtoks = [None, None]
    ttok = None
    prog.wait("gpsimd", t4)
    dtoks[0] = prog.dma("gpsimd", wt[0][:, :], Wap[0:128, :])
    for i in range(ntiles):
        b = i % 2
        if i + 1 < ntiles:
            if qtoks[(i + 1) % 2] is not None:
                prog.wait("gpsimd", qtoks[(i + 1) % 2])
            dtoks[(i + 1) % 2] = prog.dma("gpsimd", wt[(i + 1) % 2][:, :],
                                          Wap[(i + 1) * 128:(i + 2) * 128, :])
        prog.wait("vector", dtoks[b])
        prog.op("vector", lambda e, b=b: e.tensor_scalar(
            wt[b][:, :], wt[b][:, :], rqb[:, :], MAGIC, op0=ALU.mult, op1=ALU.add))
        prog.op("vector", lambda e, b=b: e.tensor_scalar(
            wt[b][:, :], wt[b][:, :], MAGIC, float(np.float32(-1.0)), op0=ALU.subtract, op1=ALU.max))
        if ttok is not None:
            prog.wait("vector", ttok)
        tq = prog.op("vector", lambda e, b=b: e.tensor_scalar(
            qb[:, :], wt[b][:, :], float(np.float32(1.0)), None, op0=ALU.min), mark="dve")
        qtoks[b] = tq
        prog.wait("tensor", tq)
        for cc in range(ncc):
            if ttok is not None:
                prog.wait("tensor", ttok)
            tt = prog.op("tensor", lambda e, cc=cc: e.transpose(
                ps_tb[:, :], qb[:, cc * 128:(cc + 1) * 128], ident_bf[:, :]), mark="pe")
            prog.wait("vector", tt)
            dst = dstT_sb[:, cc * dst_stride + i * 128: cc * dst_stride + i * 128 + 128]
            ttok = prog.op("vector", lambda e, d=dst: e.tensor_copy(d, ps_tb[:, :]), mark="dve")
    return ttok



def _prog_push_scope(self):
    self._scopes.append(ExitStack())


def _prog_pop_scope(self):
    self._scopes.pop().close()


Prog.push_scope = _prog_push_scope
Prog.pop_scope = _prog_pop_scope


def _alloc(prog):
    sc = getattr(prog, "_scopes", None)
    return sc[-1] if sc else prog.cm


def _uniq(prog, name):
    prog._uid = getattr(prog, "_uid", 0) + 1
    return f"{name}__{prog._uid}"


def sbuf(prog, name, shape, dtype):
    return _alloc(prog).enter_context(prog.nc.sbuf_tensor(_uniq(prog, name), shape, dtype))


def psum(prog, name, shape, dtype=F32):
    return _alloc(prog).enter_context(prog.nc.psum_tensor(_uniq(prog, name), shape, dtype))


def prog_init_consts(prog, io):
    prog._scopes = []
    prog.idbf = prog.cm.enter_context(prog.nc.sbuf_tensor("idbf", [128, 128], BF16))
    prog.idf32 = prog.cm.enter_context(prog.nc.sbuf_tensor("idf32", [128, 128], F32))
    prog.ones_f32 = prog.cm.enter_context(prog.nc.sbuf_tensor("ones_f32", [1, 128], F32))
    prog.ones_bf = prog.cm.enter_context(prog.nc.sbuf_tensor("ones_bf", [128, 1], BF16))
    t1 = prog.dma("sync", prog.idbf[:, :], io["c_ident_bf"][:, :])
    t2 = prog.dma("sync", prog.idf32[:, :], io["c_ident_f32"][:, :])
    t3 = prog.dma("sync", prog.ones_f32[:, :], io["c_ones"][0:1, 0:128])
    t4 = prog.dma("sync", prog.ones_bf[:, :], io["c_ones_col"][:, :])
    for e in Prog.ENGINES:
        prog.wait(e, t1)
        prog.wait(e, t2)
        prog.wait(e, t3)
        prog.wait(e, t4)



def emit_phaseD(prog, io):
    prog.push_scope()
    Wq1T = sbuf(prog, "wq1t", [128, 8 * 4096], BF16)
    Wq2T = sbuf(prog, "wq2t", [128, 32 * 1024], BF16)
    b1fm = sbuf(prog, "d_b1", [128, 32], F32)
    b2fm = sbuf(prog, "d_b2", [128, 8], F32)
    tb1 = prog.dma("sync", b1fm[:, :], io["l1f_b1fm"][:, :])
    tb2 = prog.dma("sync", b2fm[:, :], io["l1f_b2fm"][:, :])
    prog.wait("scalar", tb1)
    prog.wait("scalar", tb2)
    prog.push_scope()
    emit_quantize(prog, "q1", io["l1f_w1"], 4096, 1024, Wq1T, 4096, prog.idbf)
    emit_quantize(prog, "q2", io["l1f_w2"], 1024, 4096, Wq2T, 1024, prog.idbf)
    prog.barrier()
    prog.pop_scope()

    TBN = 256
    st1 = mk_stats(prog, "d_st1")
    st2 = mk_stats(prog, "d_st2")
    xt = [sbuf(prog, f"d_xt{i}", [128, D], F32) for i in range(2)]
    ut = sbuf(prog, "d_ut", [128, D], BF16)
    yt = sbuf(prog, "d_yt", [128, D], F32)
    ynorm = sbuf(prog, "d_yn", [128, D], F32)
    outt = [sbuf(prog, f"d_out{i}", [128, D], F32) for i in range(2)]
    u_fm = sbuf(prog, "d_ufm", [128, 8 * TBN], BF16)
    hid_fm = sbuf(prog, "d_hfm", [128, 32 * TBN], BF16)
    y_fm = sbuf(prog, "d_yfm", [128, 8 * TBN], BF16)
    ps_tr = psum(prog, "d_ptr", [128, 128], BF16)
    ps_trf = psum(prog, "d_ptrf", [128, 128], BF16)
    ps_h = [psum(prog, f"d_ph{i}", [128, TBN]) for i in range(2)]
    ps_y = [psum(prog, f"d_py{i}", [128, TBN]) for i in range(2)]

    xt_tok = [None, None]
    out_tok = [None, None]
    tr_tok = None
    trf_tok = None
    stat_guard = None

    ntb = T // TBN
    nti = TBN // 128
    for tb in range(ntb):
        for ti in range(nti):
            tile = tb * nti + ti
            b = ti % 2
            if xt_tok[b] is not None:
                prog.wait("sync", xt_tok[b])
            dtok = prog.dma("sync", xt[b][:, :], io["x_in"][tile * 128:(tile + 1) * 128, :])
            prog.wait("vector", dtok)
            prog.wait("scalar", dtok)
            if stat_guard is not None:
                prog.wait("vector", stat_guard)
                prog.wait("scalar", stat_guard)
            stok = ln_stats_tm(prog, None, xt[b], st1)
            prog.wait("vector", stok)
            apply_tok = prog.op("vector", lambda e, b=b: e.tensor_scalar(
                ut[:, :], xt[b][:, :], st1["negmean"][:, :], st1["rstd"][:, :],
                op0=ALU.add, op1=ALU.mult), mark="dve")
            stat_guard = apply_tok
            xt_tok[b] = apply_tok
            prog.wait("tensor", apply_tok)
            for dc in range(NCHUNK):
                if tr_tok is not None:
                    prog.wait("tensor", tr_tok)
                tp = prog.op("tensor", lambda e, dc=dc: e.transpose(
                    ps_tr[:, :], ut[:, dc * 128:(dc + 1) * 128], prog.idbf[:, :]), mark="pe")
                prog.wait("vector", tp)
                dst = u_fm[:, dc * TBN + ti * 128: dc * TBN + ti * 128 + 128]
                tr_tok = prog.op("vector", lambda e, d=dst: e.tensor_copy(d, ps_tr[:, :]), mark="dve")
        u_done = tr_tok
        htoks = proj_fm(prog, "d_h", lambda dc, _tb, n: u_fm[:, dc * TBN: dc * TBN + n],
                        ("sbuf", Wq1T, 4096),
                        act_evac_writer(prog, hid_fm, TBN, bias_sb=b1fm, act=AF.Relu),
                        D, 4096, TBN, rhs_tok_fn=lambda _tb: u_done, ps_banks=ps_h, tbn=TBN)
        ytoks = proj_fm(prog, "d_y", lambda fc, _tb, n: hid_fm[:, fc * TBN: fc * TBN + n],
                        ("sbuf", Wq2T, 1024),
                        act_evac_writer(prog, y_fm, TBN, bias_sb=b2fm, act=AF.Copy),
                        4096, D, TBN, rhs_tok_fn=lambda _tb: htoks[-1], ps_banks=ps_y, tbn=TBN)
        ylast = ytoks[-1]
        for ti in range(nti):
            tile = tb * nti + ti
            b = ti % 2
            prog.wait("tensor", ylast)
            for dc in range(NCHUNK):
                if trf_tok is not None:
                    prog.wait("tensor", trf_tok)
                tp = prog.op("tensor", lambda e, dc=dc, ti=ti: e.transpose(
                    ps_trf[:, :], y_fm[:, dc * TBN + ti * 128: dc * TBN + ti * 128 + 128],
                    prog.idbf[:, :]), mark="pe")
                prog.wait("vector", tp)
                trf_tok = prog.op("vector", lambda e, dc=dc: e.tensor_copy(
                    yt[:, dc * 128:(dc + 1) * 128], ps_trf[:, :]), mark="dve")
            prog.wait("vector", trf_tok)
            prog.wait("scalar", trf_tok)
            if stat_guard is not None:
                prog.wait("vector", stat_guard)
                prog.wait("scalar", stat_guard)
            stok = ln_stats_tm(prog, None, yt, st2)
            if xt_tok[b] is not None:
                prog.wait("sync", xt_tok[b])
            dtok = prog.dma("sync", xt[b][:, :], io["x_in"][tile * 128:(tile + 1) * 128, :])
            prog.wait("vector", stok)
            tn = prog.op("vector", lambda e: e.tensor_scalar(
                ynorm[:, :], yt[:, :], st2["negmean"][:, :], st2["rstd"][:, :],
                op0=ALU.add, op1=ALU.mult), mark="dve")
            prog.wait("vector", dtok)
            if out_tok[b] is not None:
                prog.wait("vector", out_tok[b])
            otok = prog.op("vector", lambda e, b=b: e.tensor_tensor(
                outt[b][:, :], xt[b][:, :], ynorm[:, :], op=ALU.add), mark="dve")
            stat_guard = otok
            xt_tok[b] = otok
            prog.wait("sync", otok)
            out_tok[b] = prog.dma("sync", io["x_out"][tile * 128:(tile + 1) * 128, :], outt[b][:, :])
    prog.pop_scope()



def host_consts():
    c = {}
    c["c_l1mask"] = host_l1mask()
    c["c_ident_bf"] = np.eye(128, dtype=ml_dtypes.bfloat16)
    c["c_ident_f32"] = np.eye(128, dtype=np.float32)
    c["c_ones"] = np.ones((1, 128), dtype=np.float32)
    c["c_ones_col"] = np.ones((128, 1), dtype=ml_dtypes.bfloat16)
    return c


def fm_vec(v):
    v = np.asarray(v, dtype=np.float32)
    return np.ascontiguousarray(v.reshape(-1, 128).T)


def declare_common(nc, names_shapes):
    aps = {}
    for name, shape, dt, kind in names_shapes:
        aps[name] = nc.dram_tensor(name, list(shape), dt, kind=kind).ap()
    return aps


def build_phaseD_program():
    nc = bass.Bass("TRN2", target_bir_lowering=False)
    decl = [
        ("x_in", (T, D), F32, "ExternalInput"),
        ("l1f_w1", (4096, 1024), F32, "ExternalInput"),
        ("l1f_w2", (1024, 4096), F32, "ExternalInput"),
        ("l1f_b1fm", (128, 32), F32, "ExternalInput"),
        ("l1f_b2fm", (128, 8), F32, "ExternalInput"),
        ("c_ident_bf", (128, 128), BF16, "ExternalInput"),
        ("c_ident_f32", (128, 128), F32, "ExternalInput"),
        ("c_ones", (1, 128), F32, "ExternalInput"),
        ("c_ones_col", (128, 1), BF16, "ExternalInput"),
        ("x_out", (T, D), F32, "ExternalOutput"),
    ]
    io = {n: nc.dram_tensor(n, list(s), dt, kind=k).ap() for n, s, dt, k in decl}
    cm = ExitStack()
    with cm:
        prog = Prog(nc, cm)
        prog_init_consts(prog, io)
        emit_phaseD(prog, io)
        if "dma_sync" in prog.sems:
            prog.wait("sync", prog.tok("dma_sync"))
        if "dma_gpsimd" in prog.sems:
            prog.wait("gpsimd", prog.tok("dma_gpsimd"))
        prog.build()
    return nc



def bcast_row(prog, name, row_ap, n, ps2, dst):
    if row_ap.tensor.__class__.__name__.startswith("DRam"):
        stage = sbuf(prog, f"bc_{name}", [1, n], F32)
        td = prog.dma("sync", stage[:, :], row_ap)
        prog.wait("tensor", td)
        row_ap = stage[:, :]
    toks = []
    for c0 in range(0, n, 512):
        nn = min(512, n - c0)
        tm = prog.op("tensor", lambda e, c0=c0, nn=nn: e.matmul(
            ps2[:, 0:nn], prog.ones_f32[:, :], row_ap[0:1, c0:c0 + nn], start=True, stop=True),
            mark="pe")
        prog.wait("vector", tm)
        tv = prog.op("vector", lambda e, c0=c0, nn=nn: e.tensor_copy(dst[:, c0:c0 + nn], ps2[:, 0:nn]),
                     mark="dve")
        prog.wait("tensor", tv)
        toks.append(tv)
    return toks[-1]


def emit_phaseC(prog, io):
    prog.push_scope()
    v_res = io["c_vres"]
    EPS = 1e-5

    st = mk_stats(prog, "c_st")
    xt = [sbuf(prog, f"c_xt{i}", [128, D], F32) for i in range(2)]
    ot = [sbuf(prog, f"c_ot{i}", [128, D], F32) for i in range(2)]
    xtok = [None, None]
    otok = [None, None]
    guard = None
    for ti in range(NTILE):
        b = ti % 2
        if xtok[b] is not None:
            prog.wait("sync", xtok[b])
        dt = prog.dma("sync", xt[b][:, :], io["x_in"][ti * 128:(ti + 1) * 128, :])
        prog.wait("vector", dt)
        prog.wait("scalar", dt)
        if guard is not None:
            prog.wait("vector", guard)
            prog.wait("scalar", guard)
        stok = ln_stats_tm(prog, None, xt[b], st)
        prog.wait("vector", stok)
        if otok[b] is not None:
            prog.wait("vector", otok[b])
        t = prog.op("vector", lambda e, b=b: e.tensor_scalar(
            ot[b][:, :], xt[b][:, :], st["negmean"][:, :], st["rstd"][:, :],
            op0=ALU.add, op1=ALU.mult), mark="dve")
        guard = t
        xtok[b] = t
        prog.wait("sync", t)
        otok[b] = prog.dma("sync", v_res[ti * 128:(ti + 1) * 128, :], ot[b][:, :])
    prog.barrier()

    for layer in range(2):
        emit_phaseC_layer(prog, io, layer)

    g_bc = sbuf(prog, "c_gbc", [128, D], F32)
    b_bc = sbuf(prog, "c_bbc", [128, D], F32)
    ps_bc = psum(prog, "c_psbc", [128, 512])
    bcast_row(prog, "gf", io["l1a_lnf_g"], D, ps_bc, g_bc)
    tbc = bcast_row(prog, "bf", io["l1a_lnf_b"], D, ps_bc, b_bc)
    prog.wait("vector", tbc)
    st2 = mk_stats(prog, "c_st2")
    vt = [sbuf(prog, f"c_vt{i}", [128, D], F32) for i in range(2)]
    yt = sbuf(prog, "c_yt", [128, D], F32)
    vtok = [None, None]
    otok = [None, None]
    guard = None
    for ti in range(NTILE):
        b = ti % 2
        if vtok[b] is not None:
            prog.wait("sync", vtok[b])
        dt = prog.dma("sync", vt[b][:, :], v_res[ti * 128:(ti + 1) * 128, :])
        prog.wait("vector", dt)
        prog.wait("scalar", dt)
        if guard is not None:
            prog.wait("vector", guard)
            prog.wait("scalar", guard)
        stok = ln_stats_tm(prog, None, vt[b], st)
        prog.wait("vector", stok)
        t1 = prog.op("vector", lambda e, b=b: e.tensor_scalar(
            yt[:, :], vt[b][:, :], st["negmean"][:, :], st["rstd"][:, :],
            op0=ALU.add, op1=ALU.mult), mark="dve")
        vtok[b] = t1
        prog.wait("vector", t1)
        prog.op("vector", lambda e: e.tensor_tensor(yt[:, :], yt[:, :], g_bc[:, :], op=ALU.mult))
        t2 = prog.op("vector", lambda e: e.tensor_tensor(yt[:, :], yt[:, :], b_bc[:, :], op=ALU.add),
                     mark="dve")
        prog.wait("vector", t2)
        prog.wait("scalar", t2)
        stok2 = ln_stats_tm(prog, None, yt, st2)
        if xtok[b] is not None:
            prog.wait("sync", xtok[b])
        dt2 = prog.dma("sync", xt[b][:, :], io["x_in"][ti * 128:(ti + 1) * 128, :])
        prog.wait("vector", stok2)
        t3 = prog.op("vector", lambda e: e.tensor_scalar(
            yt[:, :], yt[:, :], st2["negmean"][:, :], st2["rstd"][:, :],
            op0=ALU.add, op1=ALU.mult), mark="dve")
        guard = t3
        prog.wait("vector", dt2)
        prog.wait("vector", t3)
        if otok[b] is not None:
            prog.wait("vector", otok[b])
        t4 = prog.op("vector", lambda e, b=b: e.tensor_tensor(
            ot[b][:, :], xt[b][:, :], yt[:, :], op=ALU.add), mark="dve")
        guard = t4
        xtok[b] = t4
        prog.wait("sync", t4)
        otok[b] = prog.dma("sync", io["x_out"][ti * 128:(ti + 1) * 128, :], ot[b][:, :])
    prog.pop_scope()


def emit_phaseC_layer(prog, io, layer):
    v_res = io["c_vres"]
    prog.push_scope()
    attnout = sbuf(prog, "c_ao", [128, NCHUNK * T], BF16)
    prog.push_scope()
    h_fm = sbuf(prog, "c_hfm", [128, NCHUNK * T], BF16)

    sc1 = ExitStack()
    prog._scopes.append(sc1)
    g_bc = sbuf(prog, "c_g1bc", [128, D], F32)
    b_bc = sbuf(prog, "c_b1bc", [128, D], F32)
    ps_bc = psum(prog, "c_psb", [128, 512])
    bcast_row(prog, "g1", io["l1a_ln1_g"][layer:layer + 1, :], D, ps_bc, g_bc)
    tbc = bcast_row(prog, "b1", io["l1a_ln1_b"][layer:layer + 1, :], D, ps_bc, b_bc)
    prog.wait("vector", tbc)
    st = mk_stats(prog, "c_l_st")
    vt = [sbuf(prog, f"c_l_vt{i}", [128, D], F32) for i in range(2)]
    ht = sbuf(prog, "c_l_ht", [128, D], BF16)
    ps_tr = psum(prog, "c_l_ptr", [128, 128], BF16)
    vtok = [None, None]
    guard = None
    tr_tok = None
    for ti in range(NTILE):
        b = ti % 2
        if vtok[b] is not None:
            prog.wait("sync", vtok[b])
        dt = prog.dma("sync", vt[b][:, :], v_res[ti * 128:(ti + 1) * 128, :])
        prog.wait("vector", dt)
        prog.wait("scalar", dt)
        if guard is not None:
            prog.wait("vector", guard)
            prog.wait("scalar", guard)
        stok = ln_stats_tm(prog, None, vt[b], st)
        prog.wait("vector", stok)
        t1 = prog.op("vector", lambda e, b=b: e.tensor_scalar(
            ht[:, :], vt[b][:, :], st["negmean"][:, :], st["rstd"][:, :],
            op0=ALU.add, op1=ALU.mult), mark="dve")
        guard = t1
        vtok[b] = t1
        prog.op("vector", lambda e: e.tensor_tensor(ht[:, :], ht[:, :], g_bc[:, :], op=ALU.mult))
        ta = prog.op("vector", lambda e: e.tensor_tensor(ht[:, :], ht[:, :], b_bc[:, :], op=ALU.add),
                     mark="dve")
        prog.wait("tensor", ta)
        for dc in range(NCHUNK):
            if tr_tok is not None:
                prog.wait("tensor", tr_tok)
            tp = prog.op("tensor", lambda e, dc=dc: e.transpose(
                ps_tr[:, :], ht[:, dc * 128:(dc + 1) * 128], prog.idbf[:, :]), mark="pe")
            prog.wait("vector", tp)
            dst = h_fm[:, dc * T + ti * 128: dc * T + ti * 128 + 128]
            tr_tok = prog.op("vector", lambda e, d=dst: e.tensor_copy(d, ps_tr[:, :]), mark="dve")
    prog.barrier()
    prog._scopes.pop().close()

    sc2 = ExitStack()
    prog._scopes.append(sc2)
    qq = sbuf(prog, "c_qq", [128, T], BF16)
    kk = sbuf(prog, "c_kk", [128, T], BF16)
    vv = sbuf(prog, "c_vv", [128, T], BF16)
    VVa = sbuf(prog, "c_VVa", [128, 32 * 64], BF16)
    VVb = sbuf(prog, "c_VVb", [128, 32 * 64], BF16)
    P = sbuf(prog, "c_P", [128, 1024], BF16)
    rcp = sbuf(prog, "c_rcp", [1, 1024], F32)
    bc_sb = sbuf(prog, "c_bcsb", [128, 1024], F32)
    masks = sbuf(prog, "c_masks", [128, 12 * 512], BF16)
    bqs = sbuf(prog, "c_bqs", [128, 8], F32)
    bks = sbuf(prog, "c_bks", [128, 8], F32)
    bvs = sbuf(prog, "c_bvs", [128, 8], F32)
    ps_S = psum(prog, "c_psS", [128, 1024])
    ps_O = psum(prog, "c_psO", [128, 512])
    ps_denA = psum(prog, "c_psdA", [1, 512])
    ps_denB = psum(prog, "c_psdB", [1, 512])
    ps_bc2 = psum(prog, "c_psbc2", [128, 1024])
    ps_vt = psum(prog, "c_psvt", [128, 64], BF16)
    for mi in range(12):
        mtok = prog.dma("gpsimd", masks[:, mi * 512:(mi + 1) * 512],
                        io["c_l1mask"][mi * 128:(mi + 1) * 128, :])
    prog.wait("vector", mtok)
    prog.wait("tensor", mtok)
    tb_ = prog.dma("sync", bqs[:, :], io["l1a_bq_fm"][layer * 128:(layer + 1) * 128, :])
    prog.wait("scalar", tb_)
    tb_ = prog.dma("sync", bks[:, :], io["l1a_bk_fm"][layer * 128:(layer + 1) * 128, :])
    prog.wait("scalar", tb_)
    tb_ = prog.dma("sync", bvs[:, :], io["l1a_bv_fm"][layer * 128:(layer + 1) * 128, :])
    prog.wait("scalar", tb_)
    qkv_banks = [ps_bc2[:, 0:512], ps_bc2[:, 512:1024]]
    qkv_wbufs = [sbuf(prog, f"c_qkvw{i}", [128, NCHUNK * 128], BF16) for i in range(2)]

    wmap = {"q": (io["l1a_wq"], bqs, qq), "k": (io["l1a_wk"], bks, kk), "v": (io["l1a_wv"], bvs, vv)}
    h_tok = prog.tok("dve")
    exp_tok = None
    vv_guard = None
    norm_guard = None
    pv = None
    for pair in range(8):
        if norm_guard is not None:
            prog.wait("tensor", norm_guard)
        for nm in ("q", "k", "v"):
            Wap = wmap[nm][0][layer * D:(layer + 1) * D, :]
            bias_sb, dstbuf = wmap[nm][1], wmap[nm][2]

            def writer(fc, tb, pb, mm_tok, tbn_i, bias_sb=bias_sb, dstbuf=dstbuf):
                prog.wait("scalar", mm_tok)
                dst = dstbuf[:, tb * 512: tb * 512 + tbn_i]
                return prog.op("scalar", lambda e, d=dst, p=pb, n=tbn_i: e.activation(
                    out=d, in_=p[:, 0:n], func=AF.Identity,
                    bias=bias_sb[:, pair:pair + 1]), mark="act")
            prog.wait("gpsimd", prog.tok("pe"))
            toks = proj_fm(prog, f"c_{nm}", lambda dc, tb, n: h_fm[:, dc * T + tb * 512: dc * T + tb * 512 + n],
                           ("dram", Wap), writer, D, 128, T,
                           rhs_tok_fn=lambda _tb: h_tok, ps_banks=qkv_banks, fstart=pair,
                           wbufs=qkv_wbufs)
            wmap[nm] = (wmap[nm][0], bias_sb, dstbuf, toks[-1])
        if vv_guard is not None:
            prog.wait("tensor", vv_guard)
        prog.wait("tensor", wmap["v"][3])
        vt_tok = None
        for kc in range(32):
            for hh in range(2):
                VV = VVa if hh == 0 else VVb
                if vt_tok is not None:
                    prog.wait("tensor", vt_tok)
                tp = prog.op("tensor", lambda e, kc=kc, hh=hh: e.transpose(
                    ps_vt[0:128, 0:64], vv[hh * 64:(hh + 1) * 64, kc * 128:(kc + 1) * 128],
                    prog.idbf[hh * 64:(hh + 1) * 64, hh * 64:hh * 64 + 64]), mark="pe")
                prog.wait("vector", tp)
                vt_tok = prog.op("vector", lambda e, kc=kc, VV=VV: e.tensor_copy(
                    VV[:, kc * 64:(kc + 1) * 64], ps_vt[0:128, 0:64]), mark="dve")
        prog.wait("tensor", wmap["q"][3])
        prog.wait("tensor", wmap["k"][3])
        prog.wait("tensor", vt_tok)
        for (off, nq_g) in ((1, 1365), (2, 1365), (0, 1366)):
            nchunks = ceil_div(nq_g, 512)
            for ci in range(nchunks):
                j0 = ci * 512
                n = min(512, nq_g - j0)
                kcn = (3 * (j0 + n - 1)) // 128 + 1 if off == 0 else 32
                if norm_guard is not None:
                    prog.wait("tensor", norm_guard)
                for kc in range(kcn):
                    first = kc == 0
                    last = kc == kcn - 1
                    if exp_tok is not None:
                        prog.wait("tensor", exp_tok)
                    smm = None
                    for hh in range(2):
                        colap = qq[hh * 64:(hh + 1) * 64, off + 3 * j0: off + 3 * (j0 + n - 1) + 1: 3]
                        smm = prog.op("tensor", lambda e, kc=kc, hh=hh, c=colap, n=n: e.matmul(
                            ps_S[:, hh * 512: hh * 512 + n], kk[hh * 64:(hh + 1) * 64, kc * 128:(kc + 1) * 128],
                            c, start=True, stop=True), mark="pe" if hh == 1 else None)
                    if off == 0 and 128 * kc + 127 > 3 * j0:
                        t = kc - (3 * j0) // 128
                        assert 0 <= t < 12, (t, kc, j0)
                        prog.wait("vector", smm)
                        for hh in range(2):
                            smm = prog.op("vector", lambda e, t=t, n=n, hh=hh: e.tensor_tensor(
                                ps_S[:, hh * 512: hh * 512 + n], ps_S[:, hh * 512: hh * 512 + n],
                                masks[:, t * 512: t * 512 + n],
                                op=ALU.add), mark="dve")
                    prog.wait("scalar", smm)
                    if pv is not None:
                        prog.wait("scalar", pv)
                    exp_tok = prog.op("scalar", lambda e: e.activation(
                        out=P[:, :], in_=ps_S[:, :], func=AF.Exp, scale=0.125), mark="act")
                    prog.wait("tensor", exp_tok)
                    prog.op("tensor", lambda e, f=first, l=last, n=n: e.matmul(
                        ps_denA[0:1, 0:n], prog.ones_bf[:, :], P[:, 0:n], start=f, stop=l))
                    prog.op("tensor", lambda e, f=first, l=last, n=n: e.matmul(
                        ps_denB[0:1, 0:n], prog.ones_bf[:, :], P[:, 512:512 + n], start=f, stop=l))
                    prog.op("tensor", lambda e, kc=kc, f=first, l=last, n=n: e.matmul(
                        ps_O[0:64, 0:n], VVa[:, kc * 64:(kc + 1) * 64], P[:, 0:n],
                        start=f, stop=l, tile_position=(0, 0)))
                    pv = prog.op("tensor", lambda e, kc=kc, f=first, l=last, n=n: e.matmul(
                        ps_O[64:128, 0:n], VVb[:, kc * 64:(kc + 1) * 64], P[:, 512:512 + n],
                        start=f, stop=l, tile_position=(0, 64)), mark="pe")
                if norm_guard is not None:
                    prog.wait("vector", norm_guard)
                prog.wait("vector", pv)
                tr1 = prog.op("vector", lambda e, n=n: e.reciprocal(rcp[:, 0:n], ps_denA[:, 0:n]), mark="dve")
                tr1 = prog.op("vector", lambda e, n=n: e.reciprocal(rcp[:, 512:512 + n], ps_denB[:, 0:n]), mark="dve")
                prog.wait("vector", tr1)
                prog.wait("tensor", tr1)
                tb1 = None
                for c0 in (0, 512):
                    tm_ = prog.op("tensor", lambda e, c0=c0, n=n: e.matmul(
                        ps_bc2[:, c0:c0 + n], prog.ones_f32[:, :], rcp[0:1, c0:c0 + n],
                        start=True, stop=True), mark="pe")
                    prog.wait("vector", tm_)
                    tb1 = prog.op("vector", lambda e, c0=c0, n=n: e.tensor_copy(
                        bc_sb[:, c0:c0 + n], ps_bc2[:, c0:c0 + n]), mark="dve")
                prog.wait("vector", tb1)
                toks = []
                for hh in range(2):
                    dst = attnout[hh * 64:(hh + 1) * 64,
                                  pair * T + off + 3 * j0: pair * T + off + 3 * (j0 + n - 1) + 1: 3]
                    tno = prog.op("vector", lambda e, hh=hh, d=dst, n=n: e.tensor_tensor(
                        d, ps_O[hh * 64:(hh + 1) * 64, 0:n],
                        bc_sb[hh * 64:(hh + 1) * 64, hh * 512: hh * 512 + n], op=ALU.mult),
                        mark="dve")
                    toks.append(tno)
                norm_guard = toks[-1]
        vv_guard = norm_guard
    prog.barrier()
    prog._scopes.pop().close()
    prog.pop_scope()
    emit_phaseC_stage3(prog, io, layer, attnout)
    prog.pop_scope()
    emit_phaseC_stage4(prog, io, layer)


def emit_phaseC_stage3(prog, io, layer, attnout):
    v_res = io["c_vres"]
    sc3 = ExitStack()
    prog._scopes.append(sc3)
    yo_fm = sbuf(prog, "c_yo", [128, NCHUNK * T], BF16)
    bos = sbuf(prog, "c_bos", [128, 8], F32)
    tb_ = prog.dma("sync", bos[:, :], io["l1a_bo_fm"][layer * 128:(layer + 1) * 128, :])
    prog.wait("scalar", tb_)
    ao_tok = prog.tok("dve")
    otoks = proj_fm(prog, "c_o", lambda dc, tb, n: attnout[:, dc * T + tb * 512: tb * 512 + dc * T + n],
                    ("dram", io["l1a_wo"][layer * D:(layer + 1) * D, :]),
                    act_evac_writer(prog, yo_fm, T, bias_sb=bos, act=AF.Copy),
                    D, D, T, rhs_tok_fn=lambda _tb: ao_tok)
    ylast = otoks[-1]
    st = mk_stats(prog, "c_t_st")
    vt = [sbuf(prog, f"c_t_vt{i}", [128, D], F32) for i in range(2)]
    yt = sbuf(prog, "c_t_yt", [128, D], F32)
    ps_trf = psum(prog, "c_t_ptrf", [128, 128], BF16)
    vtok = [None, None]
    wtok = [None, None]
    trf_tok = None
    prog.wait("tensor", ylast)
    for ti in range(NTILE):
        b = ti % 2
        for dc in range(NCHUNK):
            if trf_tok is not None:
                prog.wait("tensor", trf_tok)
            tp = prog.op("tensor", lambda e, dc=dc, ti=ti: e.transpose(
                ps_trf[:, :], yo_fm[:, dc * T + ti * 128: dc * T + ti * 128 + 128],
                prog.idbf[:, :]), mark="pe")
            prog.wait("vector", tp)
            trf_tok = prog.op("vector", lambda e, dc=dc: e.tensor_copy(
                yt[:, dc * 128:(dc + 1) * 128], ps_trf[:, :]), mark="dve")
        if vtok[b] is not None:
            prog.wait("sync", vtok[b])
        dt = prog.dma("sync", vt[b][:, :], v_res[ti * 128:(ti + 1) * 128, :])
        prog.wait("vector", dt)
        prog.wait("vector", trf_tok)
        if wtok[b] is not None:
            prog.wait("vector", wtok[b])
        t1 = prog.op("vector", lambda e, b=b: e.tensor_tensor(
            vt[b][:, :], vt[b][:, :], yt[:, :], op=ALU.add), mark="dve")
        vtok[b] = t1
        prog.wait("sync", t1)
        wtok[b] = prog.dma("sync", v_res[ti * 128:(ti + 1) * 128, :], vt[b][:, :])
    prog.barrier()
    prog._scopes.pop().close()


def emit_phaseC_stage4(prog, io, layer):
    v_res = io["c_vres"]
    sc4 = ExitStack()
    prog._scopes.append(sc4)
    TBN = 256
    w1sb = sbuf(prog, "c_w1sb", [128, NCHUNK * 4096], BF16)
    w2sb = sbuf(prog, "c_w2sb", [128, 32 * D], BF16)
    tw1 = prog.dma("gpsimd", w1sb[:, :].rearrange("p (c f) -> p c f", f=4096),
                   io["l1a_w1"][layer * D:(layer + 1) * D, :].rearrange("(c p) f -> p c f", p=128))
    tw2 = prog.dma("gpsimd", w2sb[:, :].rearrange("p (c f) -> p c f", f=D),
                   io["l1a_w2"][layer * 4096:(layer + 1) * 4096, :].rearrange("(c p) f -> p c f", p=128))
    g_bc = sbuf(prog, "c_g2bc", [128, D], F32)
    b_bc = sbuf(prog, "c_b2bc", [128, D], F32)
    b1s = sbuf(prog, "c_b1s", [128, 32], F32)
    b2s = sbuf(prog, "c_b2s", [128, 8], F32)
    tb1_ = prog.dma("sync", b1s[:, :], io["l1a_b1_fm"][layer * 128:(layer + 1) * 128, :])
    prog.wait("scalar", tb1_)
    tb2_ = prog.dma("sync", b2s[:, :], io["l1a_b2_fm"][layer * 128:(layer + 1) * 128, :])
    prog.wait("scalar", tb2_)
    ps_bc = psum(prog, "c_t4_psb", [128, 512])
    bcast_row(prog, "g2", io["l1a_ln2_g"][layer:layer + 1, :], D, ps_bc, g_bc)
    tbc = bcast_row(prog, "b2", io["l1a_ln2_b"][layer:layer + 1, :], D, ps_bc, b_bc)
    prog.wait("vector", tbc)
    st4 = mk_stats(prog, "c_t4_st")
    xt = [sbuf(prog, f"c_t4_xt{i}", [128, D], F32) for i in range(2)]
    ht = sbuf(prog, "c_t4_ht", [128, D], BF16)
    yt = sbuf(prog, "c_t4_yt", [128, D], F32)
    h_fm = sbuf(prog, "c_t4_hfm", [128, NCHUNK * TBN], BF16)
    hid_fm = sbuf(prog, "c_t4_hid", [128, 32 * TBN], BF16)
    y_fm = sbuf(prog, "c_t4_yfm", [128, NCHUNK * TBN], BF16)
    ps_tr = psum(prog, "c_t4_ptr", [128, 128], BF16)
    ps_h = [psum(prog, f"c_t4_ph{i}", [128, TBN]) for i in range(2)]
    ps_y = [psum(prog, f"c_t4_py{i}", [128, TBN]) for i in range(2)]
    prog.wait("tensor", tw1)
    prog.wait("tensor", tw2)
    xtok = [None, None]
    wtok = [None, None]
    guard = None
    tr_tok = None
    trf_tok = None
    ylast = None
    for tb in range(T // TBN):
        for ti in range(TBN // 128):
            tile = tb * (TBN // 128) + ti
            b = ti % 2
            if xtok[b] is not None:
                prog.wait("sync", xtok[b])
            dt = prog.dma("sync", xt[b][:, :], v_res[tile * 128:(tile + 1) * 128, :])
            prog.wait("vector", dt)
            prog.wait("scalar", dt)
            if guard is not None:
                prog.wait("vector", guard)
                prog.wait("scalar", guard)
            stok = ln_stats_tm(prog, None, xt[b], st4)
            prog.wait("vector", stok)
            t1 = prog.op("vector", lambda e, b=b: e.tensor_scalar(
                ht[:, :], xt[b][:, :], st4["negmean"][:, :], st4["rstd"][:, :],
                op0=ALU.add, op1=ALU.mult), mark="dve")
            guard = t1
            xtok[b] = t1
            prog.op("vector", lambda e: e.tensor_tensor(ht[:, :], ht[:, :], g_bc[:, :], op=ALU.mult))
            ta = prog.op("vector", lambda e: e.tensor_tensor(ht[:, :], ht[:, :], b_bc[:, :], op=ALU.add),
                         mark="dve")
            prog.wait("tensor", ta)
            for dc in range(NCHUNK):
                if tr_tok is not None:
                    prog.wait("tensor", tr_tok)
                tp = prog.op("tensor", lambda e, dc=dc: e.transpose(
                    ps_tr[:, :], ht[:, dc * 128:(dc + 1) * 128], prog.idbf[:, :]), mark="pe")
                prog.wait("vector", tp)
                dst = h_fm[:, dc * TBN + ti * 128: dc * TBN + ti * 128 + 128]
                tr_tok = prog.op("vector", lambda e, d=dst: e.tensor_copy(d, ps_tr[:, :]), mark="dve")
        u_done = tr_tok
        htoks = proj_fm(prog, "c_t4_h", lambda dc, _tb, n: h_fm[:, dc * TBN: dc * TBN + n],
                        ("sbuf", w1sb, 4096),
                        act_evac_writer(prog, hid_fm, TBN, bias_sb=b1s, act=AF.Gelu),
                        D, 4096, TBN, rhs_tok_fn=lambda _tb: u_done, ps_banks=ps_h, tbn=TBN)
        ytoks = proj_fm(prog, "c_t4_y", lambda fc, _tb, n: hid_fm[:, fc * TBN: fc * TBN + n],
                        ("sbuf", w2sb, 1024),
                        act_evac_writer(prog, y_fm, TBN, bias_sb=b2s, act=AF.Copy),
                        4096, D, TBN, rhs_tok_fn=lambda _tb: htoks[-1], ps_banks=ps_y, tbn=TBN)
        ylast = ytoks[-1]
        for ti in range(TBN // 128):
            tile = tb * (TBN // 128) + ti
            b = ti % 2
            prog.wait("tensor", ylast)
            for dc in range(NCHUNK):
                if trf_tok is not None:
                    prog.wait("tensor", trf_tok)
                tp = prog.op("tensor", lambda e, dc=dc, ti=ti: e.transpose(
                    ps_tr[:, :], y_fm[:, dc * TBN + ti * 128: dc * TBN + ti * 128 + 128],
                    prog.idbf[:, :]), mark="pe")
                prog.wait("vector", tp)
                trf_tok = prog.op("vector", lambda e, dc=dc: e.tensor_copy(
                    yt[:, dc * 128:(dc + 1) * 128], ps_tr[:, :]), mark="dve")
            prog.wait("vector", trf_tok)
            if xtok[b] is not None:
                prog.wait("sync", xtok[b])
            dt = prog.dma("sync", xt[b][:, :], v_res[tile * 128:(tile + 1) * 128, :])
            prog.wait("vector", dt)
            if wtok[b] is not None:
                prog.wait("vector", wtok[b])
            t1 = prog.op("vector", lambda e, b=b: e.tensor_tensor(
                xt[b][:, :], xt[b][:, :], yt[:, :], op=ALU.add), mark="dve")
            guard = t1
            xtok[b] = t1
            prog.wait("sync", t1)
            wtok[b] = prog.dma("sync", v_res[tile * 128:(tile + 1) * 128, :], xt[b][:, :])
    prog.barrier()
    prog._scopes.pop().close()


PHASEC_DECL = [
    ("x_in", (T, D), F32, "ExternalInput"),
    ("l1a_wq", (2 * D, D), F32, "ExternalInput"),
    ("l1a_wk", (2 * D, D), F32, "ExternalInput"),
    ("l1a_wv", (2 * D, D), F32, "ExternalInput"),
    ("l1a_wo", (2 * D, D), F32, "ExternalInput"),
    ("l1a_w1", (2 * D, 4096), F32, "ExternalInput"),
    ("l1a_w2", (2 * 4096, D), F32, "ExternalInput"),
    ("l1a_ln1_g", (2, D), F32, "ExternalInput"),
    ("l1a_ln1_b", (2, D), F32, "ExternalInput"),
    ("l1a_ln2_g", (2, D), F32, "ExternalInput"),
    ("l1a_ln2_b", (2, D), F32, "ExternalInput"),
    ("l1a_lnf_g", (1, D), F32, "ExternalInput"),
    ("l1a_lnf_b", (1, D), F32, "ExternalInput"),
    ("l1a_bq_fm", (256, 8), F32, "ExternalInput"),
    ("l1a_bk_fm", (256, 8), F32, "ExternalInput"),
    ("l1a_bv_fm", (256, 8), F32, "ExternalInput"),
    ("l1a_bo_fm", (256, 8), F32, "ExternalInput"),
    ("l1a_b1_fm", (256, 32), F32, "ExternalInput"),
    ("l1a_b2_fm", (256, 8), F32, "ExternalInput"),
    ("c_l1mask", (12 * 128, 512), F32, "ExternalInput"),
    ("c_ident_bf", (128, 128), BF16, "ExternalInput"),
    ("c_ident_f32", (128, 128), F32, "ExternalInput"),
    ("c_ones", (1, 128), F32, "ExternalInput"),
    ("c_ones_col", (128, 1), BF16, "ExternalInput"),
    ("x_out", (T, D), F32, "ExternalOutput"),
]


def host_phaseC_inputs(z, consts):
    m = {
        "l1a_wq": np.ascontiguousarray(z["L1_attn_wq"].reshape(-1, D)),
        "l1a_wk": np.ascontiguousarray(z["L1_attn_wk"].reshape(-1, D)),
        "l1a_wv": np.ascontiguousarray(z["L1_attn_wv"].reshape(-1, D)),
        "l1a_wo": np.ascontiguousarray(z["L1_attn_wo"].reshape(-1, D)),
        "l1a_w1": np.ascontiguousarray(z["L1_attn_w1"].reshape(-1, 4096)),
        "l1a_w2": np.ascontiguousarray(z["L1_attn_w2"].reshape(-1, D)),
        "l1a_ln1_g": np.ascontiguousarray(z["L1_attn_ln1_g"]),
        "l1a_ln1_b": np.ascontiguousarray(z["L1_attn_ln1_b"]),
        "l1a_ln2_g": np.ascontiguousarray(z["L1_attn_ln2_g"]),
        "l1a_ln2_b": np.ascontiguousarray(z["L1_attn_ln2_b"]),
        "l1a_lnf_g": np.ascontiguousarray(z["L1_attn_lnf_g"][None, :]),
        "l1a_lnf_b": np.ascontiguousarray(z["L1_attn_lnf_b"][None, :]),
        "l1a_bq_fm": np.concatenate([fm_vec(z["L1_attn_bq"][i]) for i in range(2)], 0),
        "l1a_bk_fm": np.concatenate([fm_vec(z["L1_attn_bk"][i]) for i in range(2)], 0),
        "l1a_bv_fm": np.concatenate([fm_vec(z["L1_attn_bv"][i]) for i in range(2)], 0),
        "l1a_bo_fm": np.concatenate([fm_vec(z["L1_attn_bo"][i]) for i in range(2)], 0),
        "l1a_b1_fm": np.concatenate([fm_vec(z["L1_attn_b1"][i]) for i in range(2)], 0),
        "l1a_b2_fm": np.concatenate([fm_vec(z["L1_attn_b2"][i]) for i in range(2)], 0),
        "c_l1mask": consts["c_l1mask"],
    }
    return m


def host_l1mask():
    out = np.zeros((12, 128, 512), np.float32)
    k = np.arange(128)[:, None]
    j = np.arange(512)[None, :]
    for t in range(12):
        out[t] = np.where(128 * t + k <= 3 * j, np.float32(0), np.float32(-1e9))
    return out.reshape(12 * 128, 512)


def build_phaseC_program():
    nc = bass.Bass("TRN2", target_bir_lowering=False)
    io = {n: nc.dram_tensor(n, list(s), dt, kind=k).ap() for n, s, dt, k in PHASEC_DECL}
    io["c_vres"] = nc.dram_tensor("c_vres", [T, D], F32).ap()
    cm = ExitStack()
    with cm:
        prog = Prog(nc, cm)
        prog_init_consts(prog, io)
        emit_phaseC(prog, io)
        if "dma_sync" in prog.sems:
            prog.wait("sync", prog.tok("dma_sync"))
        if "dma_gpsimd" in prog.sems:
            prog.wait("gpsimd", prog.tok("dma_gpsimd"))
        prog.build()
    return nc




def _breg(prog, e, name, value):
    key = "_breg_" + name
    if not hasattr(prog, key):
        r = e.alloc_register(name)
        e.reg_mov(r, value)
        setattr(prog, key, r)
    return getattr(prog, key)


def emit_phaseB_attn(prog, io):
    u_res = io["b_ures"]
    prog.push_scope()
    attnout = sbuf(prog, "b_ao", [128, NCHUNK * T], BF16)
    prog.push_scope()
    h_fm = sbuf(prog, "b_hfm", [128, NCHUNK * T], BF16)

    sc1 = ExitStack()
    prog._scopes.append(sc1)
    g_bc = sbuf(prog, "b_g1bc", [128, D], F32)
    b_bc = sbuf(prog, "b_b1bc", [128, D], F32)
    ps_bc = psum(prog, "b_psb", [128, 512])
    bcast_row(prog, "bg1", io["l0f_ln1_g"], D, ps_bc, g_bc)
    tbc = bcast_row(prog, "bb1", io["l0f_ln1_b"], D, ps_bc, b_bc)
    prog.wait("vector", tbc)
    st = mk_stats(prog, "b_st")
    st2 = mk_stats(prog, "b_st2")
    xt = [sbuf(prog, f"b_xt{i}", [128, D], F32) for i in range(2)]
    ut = [sbuf(prog, f"b_ut{i}", [128, D], F32) for i in range(2)]
    ht = sbuf(prog, "b_ht", [128, D], BF16)
    ps_tr = psum(prog, "b_ptr", [128, 128], BF16)
    xtok = [None, None]
    utok = [None, None]
    udt = [None, None]
    guard = None
    tr_tok = None
    for ti in range(NTILE):
        b = ti % 2
        if xtok[b] is not None:
            prog.wait("sync", xtok[b])
        dt = prog.dma("sync", xt[b][:, :], io["x_in"][ti * 128:(ti + 1) * 128, :])
        prog.wait("vector", dt)
        prog.wait("scalar", dt)
        if guard is not None:
            prog.wait("vector", guard)
            prog.wait("scalar", guard)
        stok = ln_stats_tm(prog, None, xt[b], st)
        prog.wait("vector", stok)
        if utok[b] is not None:
            prog.wait("vector", utok[b])
        if udt[b] is not None:
            prog.wait("vector", udt[b])
        t1 = prog.op("vector", lambda e, b=b: e.tensor_scalar(
            ut[b][:, :], xt[b][:, :], st["negmean"][:, :], st["rstd"][:, :],
            op0=ALU.add, op1=ALU.mult), mark="dve")
        xtok[b] = t1
        prog.wait("sync", t1)
        udt[b] = prog.dma("sync", u_res[ti * 128:(ti + 1) * 128, :], ut[b][:, :])
        prog.wait("scalar", t1)
        stok2 = ln_stats_tm(prog, None, ut[b], st2)
        prog.wait("vector", stok2)
        t2 = prog.op("vector", lambda e, b=b: e.tensor_scalar(
            ht[:, :], ut[b][:, :], st2["negmean"][:, :], st2["rstd"][:, :],
            op0=ALU.add, op1=ALU.mult), mark="dve")
        guard = t2
        utok[b] = t2
        prog.op("vector", lambda e: e.tensor_tensor(ht[:, :], ht[:, :], g_bc[:, :], op=ALU.mult))
        ta = prog.op("vector", lambda e: e.tensor_tensor(ht[:, :], ht[:, :], b_bc[:, :], op=ALU.add),
                     mark="dve")
        prog.wait("tensor", ta)
        for dc in range(NCHUNK):
            if tr_tok is not None:
                prog.wait("tensor", tr_tok)
            tp = prog.op("tensor", lambda e, dc=dc: e.transpose(
                ps_tr[:, :], ht[:, dc * 128:(dc + 1) * 128], prog.idbf[:, :]), mark="pe")
            prog.wait("vector", tp)
            dst = h_fm[:, dc * T + ti * 128: dc * T + ti * 128 + 128]
            tr_tok = prog.op("vector", lambda e, d=dst: e.tensor_copy(d, ps_tr[:, :]), mark="dve")
    prog.barrier()
    if "dbg_u" in io:
        t = prog.dma("sync", io["dbg_u"][:, :], u_res[:, :])
        prog.wait("sync", t)
    prog._scopes.pop().close()

    if "skip_b2" in io:
        prog.pop_scope()
        prog.pop_scope()
        return
    sc2 = ExitStack()
    prog._scopes.append(sc2)
    qq = sbuf(prog, "b_qq", [128, T], BF16)
    kk = sbuf(prog, "b_kk", [128, T], BF16)
    vv = sbuf(prog, "b_vv", [128, T], BF16)
    rot = sbuf(prog, "b_rot", [128, T], BF16)
    cosT = sbuf(prog, "b_cos", [128, T], BF16)
    sinS = sbuf(prog, "b_sin", [128, T], BF16)
    VVa = sbuf(prog, "b_VVa", [128, 32 * 64], BF16)
    VVb = sbuf(prog, "b_VVb", [128, 32 * 64], BF16)
    P = sbuf(prog, "b_P", [128, 1024], BF16)
    rcp = sbuf(prog, "b_rcp", [1, 1024], F32)
    bc_sb = sbuf(prog, "b_bcsb", [128, 1024], F32)
    masks = sbuf(prog, "b_masks", [128, 4 * 512], BF16)
    bqkv = sbuf(prog, "b_bqkv", [128, 24], F32)
    ps_S = psum(prog, "b_psS", [128, 1024])
    ps_O = psum(prog, "b_psO", [128, 512])
    ps_denA = psum(prog, "b_psdA", [1, 512])
    ps_denB = psum(prog, "b_psdB", [1, 512])
    ps_bc2 = psum(prog, "b_psbc2", [128, 1024])
    ps_vt = psum(prog, "b_psvt", [128, 64], BF16)
    tcs = prog.dma("sync", cosT[:, :], io["c_rope_cos"][:, :])
    tsn = prog.dma("sync", sinS[:, :], io["c_rope_sin"][:, :])
    for mi in range(4):
        mtok = prog.dma("gpsimd", masks[:, mi * 512:(mi + 1) * 512],
                        io["c_l0mask"][mi * 128:(mi + 1) * 128, :])
    prog.wait("vector", mtok)
    prog.wait("tensor", mtok)
    prog.wait("vector", tcs)
    prog.wait("vector", tsn)
    tb_ = prog.dma("sync", bqkv[:, :], io["l0f_battn_fm"][:, :])
    prog.wait("scalar", tb_)
    qkv_banks = [ps_bc2[:, 0:512], ps_bc2[:, 512:1024]]
    qkv_wbufs = [sbuf(prog, f"b_qkvw{i}", [128, NCHUNK * 128], BF16) for i in range(2)]

    h_tok = prog.tok("dve")
    exp_tok = None
    vv_guard = None
    norm_guard = None
    pv = None
    for pair in range(8):
        if norm_guard is not None:
            prog.wait("tensor", norm_guard)
        dsts = {"q": (qq, pair), "k": (kk, 8 + pair), "v": (vv, 16 + pair)}
        last_evac = {}
        for nm in ("q", "k", "v"):
            dstbuf, fcol = dsts[nm]

            def writer(fc, tb, pb, mm_tok, tbn_i, dstbuf=dstbuf, fcol=fcol):
                prog.wait("scalar", mm_tok)
                dst = dstbuf[:, tb * 512: tb * 512 + tbn_i]
                return prog.op("scalar", lambda e, d=dst, p=pb, n=tbn_i, fcol=fcol: e.activation(
                    out=d, in_=p[:, 0:n], func=AF.Identity,
                    bias=bqkv[:, fcol:fcol + 1]), mark="act")
            prog.wait("gpsimd", prog.tok("pe"))
            toks = proj_fm(prog, f"b_{nm}", lambda dc, tb, n: h_fm[:, dc * T + tb * 512: dc * T + tb * 512 + n],
                           ("dram", io["l0f_wqkv"]), writer, D, 128, T,
                           rhs_tok_fn=lambda _tb: h_tok, ps_banks=qkv_banks, fstart=fcol,
                           wbufs=qkv_wbufs)
            last_evac[nm] = toks[-1]
        rope_done = None
        import os as _os
        _no_rope = bool(int(_os.environ.get("K_NO_ROPE", "0")))
        for buf, btok in (() if _no_rope else ((qq, last_evac["q"]), (kk, last_evac["k"]))):
            prog.wait("scalar", btok)
            becho = prog.op("scalar", lambda e, buf=buf: e.activation(
                out=rot[0:1, 0:1], in_=buf[0:1, 0:1], func=AF.Copy), mark="act")
            prog.wait("gpsimd", becho)
            prog.wait("gpsimd", btok)
            if rope_done is not None:
                prog.wait("gpsimd", rope_done)
            rts = []
            for gi, (dstr, srcr) in enumerate([(0, 32), (32, 0), (64, 96), (96, 64)]):
                rt = prog.dma("gpsimd", rot[dstr:dstr + 32, :], buf[srcr:srcr + 32, :])
                rts.append(rt)
            for rt in rts:
                prog.wait("vector", rt)
            prog.wait("vector", btok)
            prog.op("vector", lambda e: e.tensor_tensor(rot[:, :], rot[:, :], sinS[:, :], op=ALU.mult))
            prog.op("vector", lambda e, buf=buf: e.tensor_tensor(buf[:, :], buf[:, :], cosT[:, :], op=ALU.mult))
            rope_done = prog.op("vector", lambda e, buf=buf: e.tensor_tensor(
                buf[:, :], buf[:, :], rot[:, :], op=ALU.add), mark="dve")
        if rope_done is not None:
            prog.wait("tensor", rope_done)
        prog.wait("tensor", last_evac["q"])
        prog.wait("tensor", last_evac["k"])
        if vv_guard is not None:
            prog.wait("tensor", vv_guard)
        prog.wait("tensor", last_evac["v"])
        vt_tok = None
        for kc in range(32):
            for hh in range(2):
                VV = VVa if hh == 0 else VVb
                if vt_tok is not None:
                    prog.wait("tensor", vt_tok)
                tp = prog.op("tensor", lambda e, kc=kc, hh=hh: e.transpose(
                    ps_vt[0:128, 0:64], vv[hh * 64:(hh + 1) * 64, kc * 128:(kc + 1) * 128],
                    prog.idbf[hh * 64:(hh + 1) * 64, hh * 64:hh * 64 + 64]), mark="pe")
                prog.wait("vector", tp)
                vt_tok = prog.op("vector", lambda e, kc=kc, VV=VV: e.tensor_copy(
                    VV[:, kc * 64:(kc + 1) * 64], ps_vt[0:128, 0:64]), mark="dve")
        prog.wait("tensor", vt_tok)
        for ci in range(8):
            j0 = ci * 512
            n = 512
            kcn = 4 * ci + 4
            if norm_guard is not None:
                prog.wait("tensor", norm_guard)
            for kc in range(kcn):
                first = kc == 0
                last = kc == kcn - 1
                if exp_tok is not None:
                    prog.wait("tensor", exp_tok)
                smm = None
                for hh in range(2):
                    colap = qq[hh * 64:(hh + 1) * 64, j0:j0 + n]
                    smm = prog.op("tensor", lambda e, kc=kc, hh=hh, c=colap, n=n: e.matmul(
                        ps_S[:, hh * 512: hh * 512 + n], kk[hh * 64:(hh + 1) * 64, kc * 128:(kc + 1) * 128],
                        c, start=True, stop=True), mark="pe" if hh == 1 else None)
                if kc >= 4 * ci:
                    t = kc - 4 * ci
                    prog.wait("vector", smm)
                    for hh in range(2):
                        smm = prog.op("vector", lambda e, t=t, n=n, hh=hh: e.tensor_tensor(
                            ps_S[:, hh * 512: hh * 512 + n], ps_S[:, hh * 512: hh * 512 + n],
                            masks[:, t * 512: t * 512 + n], op=ALU.add), mark="dve")
                prog.wait("scalar", smm)
                if pv is not None:
                    prog.wait("scalar", pv)
                exp_tok = prog.op("scalar", lambda e: e.activation(
                    out=P[:, :], in_=ps_S[:, :], func=AF.Exp, scale=0.125), mark="act")
                prog.wait("tensor", exp_tok)
                prog.op("tensor", lambda e, f=first, l=last, n=n: e.matmul(
                    ps_denA[0:1, 0:n], prog.ones_bf[:, :], P[:, 0:n], start=f, stop=l))
                prog.op("tensor", lambda e, f=first, l=last, n=n: e.matmul(
                    ps_denB[0:1, 0:n], prog.ones_bf[:, :], P[:, 512:512 + n], start=f, stop=l))
                prog.op("tensor", lambda e, kc=kc, f=first, l=last, n=n: e.matmul(
                    ps_O[0:64, 0:n], VVa[:, kc * 64:(kc + 1) * 64], P[:, 0:n],
                    start=f, stop=l, tile_position=(0, 0)))
                pv = prog.op("tensor", lambda e, kc=kc, f=first, l=last, n=n: e.matmul(
                    ps_O[64:128, 0:n], VVb[:, kc * 64:(kc + 1) * 64], P[:, 512:512 + n],
                    start=f, stop=l, tile_position=(0, 64)), mark="pe")
            if norm_guard is not None:
                prog.wait("vector", norm_guard)
            prog.wait("vector", pv)
            tr1 = prog.op("vector", lambda e, n=n: e.reciprocal(rcp[:, 0:n], ps_denA[:, 0:n]), mark="dve")
            tr1 = prog.op("vector", lambda e, n=n: e.reciprocal(rcp[:, 512:512 + n], ps_denB[:, 0:n]), mark="dve")
            prog.wait("vector", tr1)
            prog.wait("tensor", tr1)
            tb1 = None
            for c0 in (0, 512):
                tm_ = prog.op("tensor", lambda e, c0=c0, n=n: e.matmul(
                    ps_bc2[:, c0:c0 + n], prog.ones_f32[:, :], rcp[0:1, c0:c0 + n],
                    start=True, stop=True), mark="pe")
                prog.wait("vector", tm_)
                tb1 = prog.op("vector", lambda e, c0=c0, n=n: e.tensor_copy(
                    bc_sb[:, c0:c0 + n], ps_bc2[:, c0:c0 + n]), mark="dve")
            prog.wait("vector", tb1)
            toks = []
            for hh in range(2):
                dst = attnout[hh * 64:(hh + 1) * 64, pair * T + j0: pair * T + j0 + n]
                tno = prog.op("vector", lambda e, hh=hh, d=dst, n=n: e.tensor_tensor(
                    d, ps_O[hh * 64:(hh + 1) * 64, 0:n],
                    bc_sb[hh * 64:(hh + 1) * 64, hh * 512: hh * 512 + n], op=ALU.mult),
                    mark="dve")
                toks.append(tno)
            norm_guard = toks[-1]
        vv_guard = norm_guard
    prog.barrier()
    if "dbg_aof" in io:
        t = prog.dma("sync", io["dbg_aof"][:, :], attnout[:, :])
        prog.wait("sync", t)
    prog._scopes.pop().close()
    prog.pop_scope()

    if "skip_b3" in io:
        prog.pop_scope()
        return
    sc3 = ExitStack()
    prog._scopes.append(sc3)
    yo_fm = sbuf(prog, "b_yo", [128, NCHUNK * T], BF16)
    ao_tok = prog.tok("dve")
    otoks = proj_fm(prog, "b_o", lambda dc, tb, n: attnout[:, dc * T + tb * 512: tb * 512 + dc * T + n],
                    ("dram", io["l0f_wproj"]),
                    act_evac_writer(prog, yo_fm, T, bias_sb=None, act=AF.Copy),
                    D, D, T, rhs_tok_fn=lambda _tb: ao_tok)
    ylast = otoks[-1]
    if "dbg_yof" in io:
        prog.wait("sync", ylast)
        t = prog.dma("sync", io["dbg_yof"][:, :], yo_fm[:, :])
        prog.wait("sync", t)
    vt2 = [sbuf(prog, f"b_3vt{i}", [128, D], F32) for i in range(2)]
    yt = sbuf(prog, "b_3yt", [128, D], F32)
    ps_trf = psum(prog, "b_3ptrf", [128, 128], BF16)
    vtok = [None, None]
    wtok = [None, None]
    trf_tok = None
    prog.wait("tensor", ylast)
    for ti in range(NTILE):
        b = ti % 2
        for dc in range(NCHUNK):
            if trf_tok is not None:
                prog.wait("tensor", trf_tok)
            tp = prog.op("tensor", lambda e, dc=dc, ti=ti: e.transpose(
                ps_trf[:, :], yo_fm[:, dc * T + ti * 128: dc * T + ti * 128 + 128],
                prog.idbf[:, :]), mark="pe")
            prog.wait("vector", tp)
            trf_tok = prog.op("vector", lambda e, dc=dc: e.tensor_copy(
                yt[:, dc * 128:(dc + 1) * 128], ps_trf[:, :]), mark="dve")
        if vtok[b] is not None:
            prog.wait("sync", vtok[b])
        dt = prog.dma("sync", vt2[b][:, :], u_res[ti * 128:(ti + 1) * 128, :])
        prog.wait("vector", dt)
        prog.wait("vector", trf_tok)
        if wtok[b] is not None:
            prog.wait("vector", wtok[b])
        t1 = prog.op("vector", lambda e, b=b: e.tensor_tensor(
            vt2[b][:, :], vt2[b][:, :], yt[:, :], op=ALU.add), mark="dve")
        vtok[b] = t1
        prog.wait("sync", t1)
        wtok[b] = prog.dma("sync", u_res[ti * 128:(ti + 1) * 128, :], vt2[b][:, :])
    prog.barrier()
    prog._scopes.pop().close()
    prog.pop_scope()


def emit_phaseB_moe(prog, io):
    u_res = io["b_ures"]
    CAP = 8192
    prog.push_scope()
    gh2 = io["b_gh2"]; allh2 = io["b_allh2"]
    gtm = io["b_gtm"]; allg = io["b_allg"]
    mem = io["b_mem"]; allmem = io["b_allmem"]
    inv = io["b_inv"]; ctr = io["b_ctr"]; moe = io["b_moe"]

    zt = sbuf(prog, "m_zt", [128, D], BF16)
    tz = prog.op("vector", lambda e: e.memset(zt[:, :], 0.0), mark="dve")
    prog.wait("gpsimd", tz)
    for i in range(8 * T // 128):
        prog.dma("gpsimd", ctr[i * 128:(i + 1) * 128, :], zt[:, :])
    ot = sbuf(prog, "m_oob", [128, 512], U32)
    to_ = prog.dma("sync", ot[:, :], io["c_oob"][:, :])
    prog.wait("sync", to_)
    prog.dma("sync", inv[:, :].rearrange("(p c) o -> p (c o)", p=128), ot[:, :])

    sc4 = ExitStack()
    prog._scopes.append(sc4)
    g_bc = sbuf(prog, "m_g2bc", [128, D], F32)
    b_bc = sbuf(prog, "m_b2bc", [128, D], F32)
    ps_bc = psum(prog, "m_psb", [128, 512])
    ps_bc2_ = psum(prog, "m_psb2", [128, 512])
    bcast_row(prog, "mg2", io["l0f_ln2_g"], D, ps_bc, g_bc)
    tbc = bcast_row(prog, "mb2", io["l0f_ln2_b"], D, ps_bc2_, b_bc)
    prog.wait("vector", tbc)
    if "dbg_gbc" in io:
        prog.wait("sync", tbc)
        t_ = prog.dma("sync", io["dbg_gbc"][:, :], g_bc[:, :])
        t_ = prog.dma("sync", io["dbg_bbc"][:, :], b_bc[:, :])
        prog.wait("sync", t_)
    wg = sbuf(prog, "m_wg", [128, 8 * 8], F32)
    twg = prog.dma("sync", wg[:, :], io["l0f_wgate_fm"][:, :])
    prog.wait("tensor", twg)
    st = mk_stats(prog, "m_st")
    ut2 = [sbuf(prog, f"m_ut{i}", [128, D], F32) for i in range(2)]
    h2t = sbuf(prog, "m_h2t", [128, D], F32)
    h2x = sbuf(prog, "m_h2x", [128, D], F32)
    fmt = sbuf(prog, "m_fmt", [128, 128], F32)
    gl8 = sbuf(prog, "m_gl8", [8, 128], F32)
    glt = sbuf(prog, "m_glt", [128, 8], F32)
    gex = sbuf(prog, "m_gex", [128, 8], F32)
    iota8 = sbuf(prog, "m_iota8", [128, 8], F32)
    m1 = sbuf(prog, "m_m1", [128, 1], F32)
    e1 = sbuf(prog, "m_e1", [128, 1], F32)
    oh1 = sbuf(prog, "m_oh1", [128, 8], F32)
    m2 = sbuf(prog, "m_m2", [128, 1], F32)
    e2 = sbuf(prog, "m_e2", [128, 1], F32)
    oh2 = sbuf(prog, "m_oh2", [128, 8], F32)
    e8z = sbuf(prog, "m_e8z", [128, 8], F32)
    z8r = sbuf(prog, "m_z8r", [128, 8], F32)
    msum = sbuf(prog, "m_msum", [128, 1], F32)
    mem_sb = sbuf(prog, "m_mem", [8, T], F32)
    ps_tr = psum(prog, "m_ptr", [128, 128], F32)
    ps_gl = psum(prog, "m_pgl", [8, 128])
    ps_gt = psum(prog, "m_pgt", [128, 8])
    ps_me = psum(prog, "m_pme", [8, 128])
    tio = prog.op("gpsimd", lambda e: e.iota(iota8[:, :], pattern=[[1, 8]], base=0,
                                             channel_multiplier=0,
                                             allow_small_or_imprecise_dtypes=True), mark="gp")
    prog.wait("vector", tio)
    tz8 = prog.op("vector", lambda e: e.memset(z8r[:, :], 0.0), mark="dve")
    prog.wait("vector", tz8)
    utok = [None, None]
    guard = None
    h2dma = None
    me_tok = None
    gmm_tok = None
    for ti in range(NTILE):
        b = ti % 2
        if utok[b] is not None:
            prog.wait("sync", utok[b])
        dt = prog.dma("sync", ut2[b][:, :], u_res[ti * 128:(ti + 1) * 128, :])
        prog.wait("vector", dt)
        prog.wait("scalar", dt)
        if guard is not None:
            prog.wait("vector", guard)
            prog.wait("scalar", guard)
        stok = ln_stats_tm(prog, None, ut2[b], st)
        prog.wait("vector", stok)
        if h2dma is not None:
            prog.wait("vector", h2dma)
        t1 = prog.op("vector", lambda e, b=b: e.tensor_scalar(
            h2t[:, :], ut2[b][:, :], st["negmean"][:, :], st["rstd"][:, :],
            op0=ALU.add, op1=ALU.mult), mark="dve")
        guard = t1
        utok[b] = t1
        if "dbg_base" in io and ti == 0:
            prog.wait("sync", t1)
            tdb = prog.dma("sync", io["dbg_base"][:, :], h2t[:, :])
            tdb = prog.dma("sync", io["dbg_nm"][:, :], st["negmean"][:, :])
            tdb = prog.dma("sync", io["dbg_rs"][:, :], st["rstd"][:, :])
            tdb = prog.dma("sync", io["dbg_ut"][:, :], ut2[b][:, :])
            prog.wait("vector", tdb)
        tg_ = prog.op("vector", lambda e: e.tensor_tensor(h2x[:, :], h2t[:, :], g_bc[:, :], op=ALU.mult),
                      mark="dve")
        prog.wait("vector", tg_)
        ta = prog.op("vector", lambda e: e.tensor_tensor(h2t[:, :], h2x[:, :], b_bc[:, :], op=ALU.add),
                     mark="dve")
        prog.wait("gpsimd", ta)
        h2dma = prog.dma("gpsimd", gh2[ti * 128:(ti + 1) * 128, :], h2t[:, :])
        prog.wait("tensor", ta)
        tr_tok = None
        for dc in range(NCHUNK):
            if tr_tok is not None:
                prog.wait("tensor", tr_tok)
            tp = prog.op("tensor", lambda e, dc=dc: e.transpose(
                ps_tr[:, :], h2t[:, dc * 128:(dc + 1) * 128], prog.idf32[:, :]), mark="pe")
            prog.wait("vector", tp)
            if gmm_tok is not None:
                prog.wait("vector", gmm_tok)
            tr_tok = prog.op("vector", lambda e: e.tensor_copy(fmt[:, :], ps_tr[:, :]), mark="dve")
            prog.wait("tensor", tr_tok)
            gmm_tok = prog.op("tensor", lambda e, dc=dc: e.matmul(
                ps_gl[:, :], wg[:, dc * 8:(dc + 1) * 8], fmt[:, :],
                start=(dc == 0), stop=(dc == NCHUNK - 1)), mark="pe")
        prog.wait("vector", gmm_tok)
        tev = prog.op("vector", lambda e: e.tensor_copy(gl8[:, :], ps_gl[:, :]), mark="dve")
        prog.wait("tensor", tev)
        tgt = prog.op("tensor", lambda e: e.transpose(
            ps_gt[:, :], gl8[:, :], prog.idf32[0:8, 0:8]), mark="pe")
        prog.wait("vector", tgt)
        t5 = prog.op("vector", lambda e: e.tensor_copy(glt[:, :], ps_gt[:, :]), mark="dve")
        prog.wait("vector", t5)
        t6 = prog.op("vector", lambda e: e.reduce_max(m1[:, :], glt[:, :], axis=AX.X), mark="dve")
        prog.wait("vector", t6)
        t6 = prog.op("vector", lambda e: e.tensor_scalar(m1[:, :], m1[:, :], -1.0, None,
                                                         op0=ALU.mult), mark="dve")
        prog.wait("vector", t6)
        prog.wait("scalar", t6)
        t7 = prog.op("scalar", lambda e: e.activation(out=gex[:, :], in_=glt[:, :], func=AF.Exp,
                                                      bias=m1[:, :]), mark="act")
        prog.wait("vector", t7)
        t8 = prog.op("vector", lambda e: e.reduce_sum(msum[:, :], gex[:, :], axis=AX.X), mark="dve")
        prog.wait("vector", t8)
        t9 = prog.op("vector", lambda e: e.reciprocal(msum[:, :], msum[:, :]), mark="dve")
        prog.wait("vector", t9)
        t10 = prog.op("vector", lambda e: e.tensor_scalar(
            gex[:, :], gex[:, :], msum[:, :], None, op0=ALU.mult), mark="dve")
        prog.wait("vector", t10)
        prog.wait("sync", t10)
        gdma = prog.dma("sync", gtm[ti * 128:(ti + 1) * 128, :], gex[:, :])
        def first_onehot(src_buf, oh_buf):
            ta1 = prog.op("vector", lambda e: e.reduce_max(m1[:, :], src_buf[:, :], axis=AX.X),
                          mark="dve")
            prog.wait("vector", ta1)
            ta2 = prog.op("vector", lambda e: e.tensor_scalar(
                oh_buf[:, :], src_buf[:, :], m1[:, :], None, op0=ALU.is_equal), mark="dve")
            prog.wait("vector", ta2)
            ta3 = prog.op("vector", lambda e: e.tensor_tensor_scan(
                e8z[:, :], oh_buf[:, :], z8r[:, :], 0.0, op0=ALU.add, op1=ALU.add), mark="dve")
            prog.wait("vector", ta3)
            ta4 = prog.op("vector", lambda e: e.tensor_scalar(
                e8z[:, :], e8z[:, :], 1.0, None, op0=ALU.is_equal), mark="dve")
            prog.wait("vector", ta4)
            ta5 = prog.op("vector", lambda e: e.tensor_tensor(
                oh_buf[:, :], oh_buf[:, :], e8z[:, :], op=ALU.mult), mark="dve")
            prog.wait("vector", ta5)
            return ta5
        first_onehot(gex, oh1)
        t14 = prog.op("vector", lambda e: e.scalar_tensor_tensor(
            glt[:, :], oh1[:, :], -4.0, gex[:, :], op0=ALU.mult, op1=ALU.add), mark="dve")
        prog.wait("vector", t14)
        first_onehot(glt, oh2)
        t18 = prog.op("vector", lambda e: e.tensor_tensor(oh1[:, :], oh1[:, :], oh2[:, :], op=ALU.add),
                      mark="dve")
        prog.wait("tensor", t18)
        if me_tok is not None:
            prog.wait("tensor", me_tok)
        t19 = prog.op("tensor", lambda e: e.transpose(ps_me[:, :], oh1[:, :], prog.idf32[:, :]),
                      mark="pe")
        prog.wait("vector", t19)
        me_tok = prog.op("vector", lambda e, ti=ti: e.tensor_copy(
            mem_sb[:, ti * 128:(ti + 1) * 128], ps_me[:, :]), mark="dve")
        guard = me_tok
    prog.wait("sync", me_tok)
    prog.dma("sync", mem[:, :], mem_sb[:, :])
    prog.wait("gpsimd", prog.tok("dma_sync"))
    prog.wait("gpsimd", prog.tok("dma_gpsimd"))
    if "b4_stop" in io:
        prog.barrier()
        prog._scopes.pop().close()
        prog.pop_scope()
        return
    cgate = prog.op("gpsimd", lambda e: e.collective_compute(
        "AllGather", ALU.bypass, replica_groups=[list(range(8))],
        ins=[gtm.opt()], outs=[allg.opt()]), mark="cc")
    cmask = prog.op("gpsimd", lambda e: e.collective_compute(
        "AllGather", ALU.bypass, replica_groups=[list(range(8))],
        ins=[mem.opt()], outs=[allmem.opt()]), mark="cc")
    ch2 = prog.op("gpsimd", lambda e: e.collective_compute(
        "AllGather", ALU.bypass, replica_groups=[list(range(8))],
        ins=[gh2.opt()], outs=[allh2.opt()]), mark="cc")
    prog.barrier()
    prog._scopes.pop().close()

    sc5 = ExitStack()
    prog._scopes.append(sc5)
    esel = sbuf(prog, "m_esel", [128, 8], F32)
    ebase = sbuf(prog, "m_ebase", [128, 1], F32)
    oobrow = sbuf(prog, "m_oobr", [128, 1], F32)
    te_ = prog.dma("sync", esel[:, :], io["c_esel"][:, :])
    tb2_ = prog.dma("sync", ebase[:, :], io["c_ebase"][:, :])
    to2_ = prog.dma("sync", oobrow[:, :], io["c_oobrow"][:, :])
    cem = sbuf(prog, "m_cem", [8, T], F32)
    gpos = sbuf(prog, "m_gpos", [8, T], F32)
    zrow = sbuf(prog, "m_zrow", [8, T], F32)
    carry = sbuf(prog, "m_carry", [8, 1], F32)
    ps_g2 = psum(prog, "m_pg2", [128, 8])
    gposTM = sbuf(prog, "m_gposTM", [128, 8], F32)
    maskTM = sbuf(prog, "m_maskTM", [128, 8], F32)
    gjunk2 = sbuf(prog, "m_gjk2", [128, 8], F32)
    gpec = sbuf(prog, "m_gpec", [128, 1], F32)
    mskc = sbuf(prog, "m_mskc", [128, 1], F32)
    kpt = sbuf(prog, "m_kpt", [128, 1], F32)
    idxf2 = sbuf(prog, "m_idxf", [128, 1], F32)
    idxu = sbuf(prog, "m_idxu", [128, 1], U32)
    idt = sbuf(prog, "m_idt", [128, 1], F32)
    idt8 = sbuf(prog, "m_idt8", [128, 8], F32)
    idtu = sbuf(prog, "m_idtu", [128, 8], U32)
    z8r2 = sbuf(prog, "m_z8r2", [128, 8], F32)
    iotacol = sbuf(prog, "m_iotac", [128, 1], F32)
    tio2 = prog.op("gpsimd", lambda e: e.iota(iotacol[:, :], pattern=[[0, 1]], base=0,
                                              channel_multiplier=1,
                                              allow_small_or_imprecise_dtypes=True), mark="gp")
    prog.wait("sync", ("cc", 2))
    tz2 = prog.op("vector", lambda e: e.memset(zrow[:, :], 0.0), mark="dve")
    tz3 = prog.op("vector", lambda e: e.memset(carry[:, :], 0.0), mark="dve")
    prog.wait("vector", tz2)
    prog.wait("vector", tz3)
    prog.wait("vector", te_)
    prog.wait("vector", tb2_)
    prog.wait("vector", to2_)
    prog.wait("vector", tio2)
    tz8b = prog.op("vector", lambda e: e.memset(z8r2[:, :], 0.0), mark="dve")
    prog.wait("vector", tz8b)
    gp_tok = None
    sc_tok = None
    for cch in range(8):
        if gp_tok is not None:
            prog.wait("sync", gp_tok)
        tcm = None
        for e8 in range(8):
            tcm = prog.dma("sync", cem[e8:e8 + 1, :], allmem[cch * 8 + e8:cch * 8 + e8 + 1, :])
        prog.wait("vector", tcm)
        prog.wait("vector", prog.tok("dma_sync"))
        if gp_tok is not None:
            prog.wait("vector", gp_tok)
        tsc = prog.op("vector", lambda e: e.tensor_tensor_scan(
            gpos[:, :], cem[:, :], zrow[:, :], carry[:, :], op0=ALU.add, op1=ALU.add), mark="dve")
        prog.wait("vector", tsc)
        tcr = prog.op("vector", lambda e: e.tensor_copy(carry[:, :], gpos[:, T - 1:T]), mark="dve")
        prog.wait("vector", tcr)
        tgp = prog.op("vector", lambda e: e.tensor_scalar(
            gpos[:, :], gpos[:, :], -1.0, None, op0=ALU.add), mark="dve")
        prog.wait("vector", tgp)
        prog.wait("tensor", tgp)
        for ti in range(NTILE):
            gt = cch * NTILE + ti
            i0 = ti * 128
            if gp_tok is not None:
                prog.wait("tensor", gp_tok)
            tq1 = prog.op("tensor", lambda e, i0=i0: e.transpose(
                ps_g2[:, :], gpos[:, i0:i0 + 128], prog.idf32[0:8, 0:8]), mark="pe")
            prog.wait("vector", tq1)
            tq2 = prog.op("vector", lambda e: e.tensor_copy(gposTM[:, :], ps_g2[:, :]), mark="dve")
            prog.wait("tensor", tq2)
            tq3 = prog.op("tensor", lambda e, i0=i0: e.transpose(
                ps_g2[:, :], cem[:, i0:i0 + 128], prog.idf32[0:8, 0:8]), mark="pe")
            gp_tok = tq3
            prog.wait("vector", tq3)
            tq4 = prog.op("vector", lambda e: e.tensor_copy(maskTM[:, :], ps_g2[:, :]), mark="dve")
            prog.wait("vector", tq4)
            tq5a = prog.op("vector", lambda e: e.tensor_tensor(
                gjunk2[:, :], gposTM[:, :], esel[:, :], op=ALU.mult), mark="dve")
            prog.wait("vector", tq5a)
            tq5 = prog.op("vector", lambda e: e.reduce_sum(gpec[:, :], gjunk2[:, :], axis=AX.X),
                          mark="dve")
            prog.wait("vector", tq5)
            tq6a = prog.op("vector", lambda e: e.tensor_tensor(
                gjunk2[:, :], maskTM[:, :], esel[:, :], op=ALU.mult), mark="dve")
            prog.wait("vector", tq6a)
            tq6 = prog.op("vector", lambda e: e.reduce_sum(mskc[:, :], gjunk2[:, :], axis=AX.X),
                          mark="dve")
            prog.wait("vector", tq6)
            tq7 = prog.op("vector", lambda e: e.tensor_scalar(
                kpt[:, :], gpec[:, :], float(CAP), None, op0=ALU.is_lt), mark="dve")
            prog.wait("vector", tq7)
            tq8 = prog.op("vector", lambda e: e.tensor_tensor(kpt[:, :], kpt[:, :], mskc[:, :],
                                                              op=ALU.mult), mark="dve")
            prog.wait("vector", tq8)
            tq9 = prog.op("vector", lambda e: e.tensor_scalar(
                idxf2[:, :], gpec[:, :], oobrow[:, :], None, op0=ALU.subtract),
                mark="dve")
            prog.wait("vector", tq9)
            tq10 = prog.op("vector", lambda e: e.scalar_tensor_tensor(
                idxf2[:, :], idxf2[:, :], kpt[:, :], oobrow[:, :], op0=ALU.mult, op1=ALU.add), mark="dve")
            prog.wait("vector", tq10)
            if sc_tok is not None:
                prog.wait("vector", sc_tok)
            tq11 = prog.op("vector", lambda e: e.tensor_copy(idxu[:, :], idxf2[:, :]), mark="dve")
            tq12 = prog.op("vector", lambda e, gt=gt: e.tensor_scalar(
                idt[:, :], iotacol[:, :], float(gt * 128), None, op0=ALU.add), mark="dve")
            prog.wait("vector", tq12)
            tq12b = prog.op("vector", lambda e: e.tensor_scalar(
                idt8[:, :], z8r2[:, :], idt[:, :], None, op0=ALU.add), mark="dve")
            prog.wait("vector", tq12b)
            tq13 = prog.op("vector", lambda e: e.tensor_copy(idtu[:, :], idt8[:, :]), mark="dve")
            prog.wait("gpsimd", tq11)
            prog.wait("gpsimd", tq13)
            if sc_tok is not None:
                prog.wait("gpsimd", sc_tok)
            sc_tok = prog.op("gpsimd", lambda e: e.indirect_dma_start(
                out=inv[:, :], out_offset=IndirectOffsetOnAxis(idxu[:, :], 0),
                in_=idtu[:, :], in_offset=None,
                bounds_check=_breg(prog, e, "b8191", CAP - 1), oob_is_err=False),
                mark="dma_gpsimd", inc=16)
            gp_tok = sc_tok if False else gp_tok
    prog.wait("gpsimd", sc_tok)
    prog.wait("gpsimd", ("cc", 3))
    prog.wait("gpsimd", prog.tok("dma_gpsimd"))
    prog.barrier()
    if "b5_stop" in io:
        if "d_inv5" in io:
            t = prog.dma("sync", io["d_inv5"][:, :], inv[:, :])
            prog.wait("sync", t)
        if "d_mem5" in io:
            t = prog.dma("sync", io["d_mem5"][:, :], allmem[:, :])
            t = prog.dma("sync", io["d_allg5"][:, :], allg[:, :])
            prog.wait("sync", t)
        prog._scopes.pop().close()
        prog.pop_scope()
        return
    prog._scopes.pop().close()
    prog.push_scope()

    TBN = 256
    w1sb = sbuf(prog, "m_w1sb", [128, NCHUNK * 4096], BF16)
    w2sb = sbuf(prog, "m_w2sb", [128, 32 * D], BF16)
    tw1 = prog.dma("gpsimd", w1sb[:, :].rearrange("p (c f) -> p c f", f=4096),
                   io["moe_w1"][:, :].rearrange("(c p) f -> p c f", p=128))
    tw2 = prog.dma("gpsimd", w2sb[:, :].rearrange("p (c f) -> p c f", f=D),
                   io["moe_w2"][:, :].rearrange("(c p) f -> p c f", p=128))
    b1s = sbuf(prog, "m_b1s", [128, 32], F32)
    b2s = sbuf(prog, "m_b2s", [128, 8], F32)
    tb1_ = prog.dma("sync", b1s[:, :], io["moe_b1fm"][:, :])
    prog.wait("scalar", tb1_)
    tb2_ = prog.dma("sync", b2s[:, :], io["moe_b2fm"][:, :])
    prog.wait("scalar", tb2_)
    esel6 = sbuf(prog, "m_esel2", [128, 8], F32)
    te_ = prog.dma("sync", esel6[:, :], io["c_esel"][:, :])
    prog.wait("vector", te_)
    ivs = [sbuf(prog, f"m_ivs{i}", [128, 8], U32) for i in range(2)]
    xtm = [sbuf(prog, f"m_xtm{i}", [128, D], BF16) for i in range(2)]
    gtile = [sbuf(prog, f"m_gt{i}", [128, 8], F32) for i in range(2)]
    gsel = [sbuf(prog, f"m_gs{i}", [128, 1], F32) for i in range(2)]
    gjunk = sbuf(prog, "m_gjunk", [128, 8], F32)
    ytm = [sbuf(prog, f"m_ytm{i}", [128, D], BF16) for i in range(2)]
    xin_fm = sbuf(prog, "m_xin", [128, NCHUNK * TBN], BF16)
    hid_fm = sbuf(prog, "m_hid", [128, 32 * TBN], BF16)
    y_fm = sbuf(prog, "m_yfm", [128, NCHUNK * TBN], BF16)
    ps_tr6 = psum(prog, "m_ptr2", [128, 128], BF16)
    ps_h = [psum(prog, f"m_ph{i}", [128, TBN]) for i in range(2)]
    ps_y = [psum(prog, f"m_py{i}", [128, TBN]) for i in range(2)]
    prog.wait("tensor", tw1)
    prog.wait("tensor", tw2)
    tr_tok = None
    trf_tok = None
    gstok = [None, None]
    ytok = [None, None]
    for tb in range(CAP // TBN):
        for half in range(2):
            slot0 = tb * TBN + half * 128
            if ytok[half] is not None:
                prog.wait("sync", ytok[half])
            tiv = prog.dma("sync", ivs[half][:, :], inv[slot0:slot0 + 128, :])
            tmz = prog.op("vector", lambda e, half=half: e.memset(xtm[half][:, :], 0.0), mark="dve")
            tgz = prog.op("vector", lambda e, half=half: e.memset(gtile[half][:, :], 0.0), mark="dve")
            prog.wait("gpsimd", tiv)
            prog.wait("gpsimd", tmz)
            prog.wait("gpsimd", tgz)
            tg1 = prog.op("gpsimd", lambda e, half=half: e.indirect_dma_start(
                out=xtm[half][:, :], out_offset=None, in_=allh2[:, :],
                in_offset=IndirectOffsetOnAxis(ivs[half][:, 0:1], 0),
                bounds_check=_breg(prog, e, "b32767", 8 * T - 1), oob_is_err=False),
                mark="dma_gpsimd", inc=16)
            tg2 = prog.op("gpsimd", lambda e, half=half: e.indirect_dma_start(
                out=gtile[half][:, :], out_offset=None, in_=allg[:, :],
                in_offset=IndirectOffsetOnAxis(ivs[half][:, 0:1], 0),
                bounds_check=_breg(prog, e, "b32767", 8 * T - 1), oob_is_err=False),
                mark="dma_gpsimd", inc=16)
            prog.wait("vector", tg2)
            tgsa = prog.op("vector", lambda e, half=half: e.tensor_tensor(
                gjunk[:, :], gtile[half][:, :], esel6[:, :], op=ALU.mult), mark="dve")
            prog.wait("vector", tgsa)
            tgs = prog.op("vector", lambda e, half=half: e.reduce_sum(
                gsel[half][:, :], gjunk[:, :], axis=AX.X), mark="dve")
            prog.wait("vector", tgs)
            gstok[half] = tgs
            prog.wait("tensor", tg1)
            for dc in range(NCHUNK):
                if tr_tok is not None:
                    prog.wait("tensor", tr_tok)
                tp = prog.op("tensor", lambda e, dc=dc, half=half: e.transpose(
                    ps_tr6[:, :], xtm[half][:, dc * 128:(dc + 1) * 128], prog.idbf[:, :]), mark="pe")
                prog.wait("vector", tp)
                dst = xin_fm[:, dc * TBN + half * 128: dc * TBN + half * 128 + 128]
                tr_tok = prog.op("vector", lambda e, d=dst: e.tensor_copy(d, ps_tr6[:, :]), mark="dve")
        u_done = tr_tok
        htoks = proj_fm(prog, "m_h", lambda dc, _tb, n: xin_fm[:, dc * TBN: dc * TBN + n],
                        ("sbuf", w1sb, 4096),
                        act_evac_writer(prog, hid_fm, TBN, bias_sb=b1s, act=AF.Gelu),
                        D, 4096, TBN, rhs_tok_fn=lambda _tb: u_done, ps_banks=ps_h, tbn=TBN)
        ytoks = proj_fm(prog, "m_y", lambda fc, _tb, n: hid_fm[:, fc * TBN: fc * TBN + n],
                        ("sbuf", w2sb, 1024),
                        act_evac_writer(prog, y_fm, TBN, bias_sb=b2s, act=AF.Copy),
                        4096, D, TBN, rhs_tok_fn=lambda _tb: htoks[-1], ps_banks=ps_y, tbn=TBN)
        ylast = ytoks[-1]
        prog.wait("tensor", ylast)
        for half in range(2):
            if ytok[half] is not None:
                prog.wait("vector", ytok[half])
            for dc in range(NCHUNK):
                if trf_tok is not None:
                    prog.wait("tensor", trf_tok)
                tp = prog.op("tensor", lambda e, dc=dc, half=half: e.transpose(
                    ps_tr6[:, :], y_fm[:, dc * TBN + half * 128: dc * TBN + half * 128 + 128],
                    prog.idbf[:, :]), mark="pe")
                prog.wait("vector", tp)
                trf_tok = prog.op("vector", lambda e, dc=dc, half=half: e.tensor_copy(
                    ytm[half][:, dc * 128:(dc + 1) * 128], ps_tr6[:, :]), mark="dve")
            prog.wait("vector", trf_tok)
            prog.wait("vector", gstok[half])
            tsc2 = prog.op("vector", lambda e, half=half: e.tensor_scalar(
                ytm[half][:, :], ytm[half][:, :], gsel[half][:, :], None, op0=ALU.mult), mark="dve")
            prog.wait("gpsimd", tsc2)
            ytok[half] = prog.op("gpsimd", lambda e, half=half: e.indirect_dma_start(
                out=ctr[:, :], out_offset=IndirectOffsetOnAxis(ivs[half][:, 0:1], 0),
                in_=ytm[half][:, :], in_offset=None,
                bounds_check=_breg(prog, e, "b32767", 8 * T - 1), oob_is_err=False),
                mark="dma_gpsimd", inc=16)
    prog.wait("gpsimd", prog.tok("dma_gpsimd"))
    prog.barrier()
    prog.pop_scope()
    prog.op("gpsimd", lambda e: e.collective_compute(
        "ReduceScatter", ALU.add, replica_groups=[list(range(8))],
        ins=[ctr.opt()], outs=[moe.opt()]), mark="cc")
    prog.wait("sync", prog.tok("cc"))
    prog.push_scope()
    st7 = mk_stats(prog, "m7_st")
    ut3 = [sbuf(prog, f"m7_ut{i}", [128, D], F32) for i in range(2)]
    mt = [sbuf(prog, f"m7_mt{i}", [128, D], BF16) for i in range(2)]
    xt3 = [sbuf(prog, f"m7_xt{i}", [128, D], F32) for i in range(2)]
    ot3 = [sbuf(prog, f"m7_ot{i}", [128, D], F32) for i in range(2)]
    utok = [None, None]
    mtok2 = [None, None]
    xtok = [None, None]
    otok = [None, None]
    guard = None
    for ti in range(NTILE):
        b = ti % 2
        if utok[b] is not None:
            prog.wait("sync", utok[b])
        du = prog.dma("sync", ut3[b][:, :], u_res[ti * 128:(ti + 1) * 128, :])
        if mtok2[b] is not None:
            prog.wait("sync", mtok2[b])
        dm = prog.dma("sync", mt[b][:, :], moe[ti * 128:(ti + 1) * 128, :])
        prog.wait("vector", du)
        prog.wait("vector", dm)
        if guard is not None:
            prog.wait("vector", guard)
            prog.wait("scalar", guard)
        t1 = prog.op("vector", lambda e, b=b: e.tensor_tensor(
            ut3[b][:, :], ut3[b][:, :], mt[b][:, :], op=ALU.add), mark="dve")
        mtok2[b] = t1
        prog.wait("scalar", t1)
        prog.wait("vector", t1)
        stok = ln_stats_tm(prog, None, ut3[b], st7)
        prog.wait("vector", stok)
        if xtok[b] is not None:
            prog.wait("sync", xtok[b])
        dx = prog.dma("sync", xt3[b][:, :], io["x_in"][ti * 128:(ti + 1) * 128, :])
        t2 = prog.op("vector", lambda e, b=b: e.tensor_scalar(
            ut3[b][:, :], ut3[b][:, :], st7["negmean"][:, :], st7["rstd"][:, :],
            op0=ALU.add, op1=ALU.mult), mark="dve")
        guard = t2
        utok[b] = t2
        prog.wait("vector", dx)
        prog.wait("vector", t2)
        if otok[b] is not None:
            prog.wait("vector", otok[b])
        t3 = prog.op("vector", lambda e, b=b: e.tensor_tensor(
            ot3[b][:, :], xt3[b][:, :], ut3[b][:, :], op=ALU.add), mark="dve")
        xtok[b] = t3
        prog.wait("sync", t3)
        otok[b] = prog.dma("sync", io["x_out"][ti * 128:(ti + 1) * 128, :], ot3[b][:, :])
    prog.pop_scope()
    prog.pop_scope()


def emit_phaseB(prog, io):
    emit_phaseB_attn(prog, io)
    prog.barrier()
    emit_phaseB_moe(prog, io)


PHASEB_DECL = [
    ("x_in", (T, D), F32, "ExternalInput"),
    ("l0f_wqkv", (D, 3 * D), F32, "ExternalInput"),
    ("l0f_wproj", (D, D), F32, "ExternalInput"),
    ("l0f_ln1_g", (1, D), F32, "ExternalInput"),
    ("l0f_ln1_b", (1, D), F32, "ExternalInput"),
    ("l0f_ln2_g", (1, D), F32, "ExternalInput"),
    ("l0f_ln2_b", (1, D), F32, "ExternalInput"),
    ("l0f_wgate_fm", (128, 64), F32, "ExternalInput"),
    ("l0f_battn_fm", (128, 24), F32, "ExternalInput"),
    ("moe_w1", (D, 4096), F32, "ExternalInput"),
    ("moe_w2", (4096, D), F32, "ExternalInput"),
    ("moe_b1fm", (128, 32), F32, "ExternalInput"),
    ("moe_b2fm", (128, 8), F32, "ExternalInput"),
    ("c_rope_cos", (128, T), BF16, "ExternalInput"),
    ("c_rope_sin", (128, T), BF16, "ExternalInput"),
    ("c_l0mask", (4 * 128, 512), F32, "ExternalInput"),
    ("c_esel", (128, 8), F32, "ExternalInput"),
    ("c_ebase", (128, 1), F32, "ExternalInput"),
    ("c_oobrow", (128, 1), F32, "ExternalInput"),
    ("c_oob", (128, 512), U32, "ExternalInput"),
    ("c_ident_bf", (128, 128), BF16, "ExternalInput"),
    ("c_ident_f32", (128, 128), F32, "ExternalInput"),
    ("c_ones", (1, 128), F32, "ExternalInput"),
    ("c_ones_col", (128, 1), BF16, "ExternalInput"),
    ("x_out", (T, D), F32, "ExternalOutput"),
]


def declare_phaseB_scratch(nc, io):
    io["b_ures"] = nc.dram_tensor("b_ures", [T, D], F32).ap()
    io["b_gh2"] = nc.dram_tensor("b_gh2", [T, D], BF16).ap()
    io["b_allh2"] = nc.dram_tensor("b_allh2", [8 * T, D], BF16).ap()
    io["b_gtm"] = nc.dram_tensor("b_gtm", [T, 8], F32).ap()
    io["b_allg"] = nc.dram_tensor("b_allg", [8 * T, 8], F32).ap()
    io["b_mem"] = nc.dram_tensor("b_mem", [8, T], F32).ap()
    io["b_allmem"] = nc.dram_tensor("b_allmem", [64, T], F32).ap()
    io["b_inv"] = nc.dram_tensor("b_inv", [8192, 8], U32).ap()
    io["b_ctr"] = nc.dram_tensor("b_ctr", [8 * T, D], BF16).ap()
    io["b_moe"] = nc.dram_tensor("b_moe", [T, D], BF16).ap()


def host_phaseB_inputs(z, consts, core):
    H = 16
    hd = 64
    half = hd // 2
    inv_freq = 1.0 / 10000.0 ** (np.arange(0, hd, 2, dtype=np.float32) / hd)
    ang = np.arange(T, dtype=np.float32)[:, None] * inv_freq
    cos = np.cos(ang)
    sin = np.sin(ang)
    p = np.arange(128)
    cosT = cos[:, p % 32].T.astype(ml_dtypes.bfloat16)
    sgn = np.where((p % 64) < 32, -1.0, 1.0).astype(np.float32)
    sinT = (sin[:, p % 32] * sgn[None, :]).T.astype(ml_dtypes.bfloat16)
    l0m = np.zeros((4, 128, 512), np.float32)
    k = np.arange(128)[:, None]
    j = np.arange(512)[None, :]
    for t in range(4):
        l0m[t] = np.where(128 * t + k <= j, np.float32(0), np.float32(-1e9))
    esel = np.zeros((128, 8), np.float32)
    esel[:, core] = 1.0
    perm = np.concatenate([np.arange(h * 192 + nm * 64, h * 192 + nm * 64 + 64)
                           for nm in range(3) for h in range(16)])
    m = {
        "l0f_wqkv": np.ascontiguousarray(np.asarray(z["L0_ffn_Wqkv"])[:, perm]),
        "l0f_wproj": z["L0_ffn_Wproj"],
        "l0f_ln1_g": z["L0_ffn_ln1_g"][None, :],
        "l0f_ln1_b": z["L0_ffn_ln1_b"][None, :],
        "l0f_ln2_g": z["L0_ffn_ln2_g"][None, :],
        "l0f_ln2_b": z["L0_ffn_ln2_b"][None, :],
        "l0f_wgate_fm": np.ascontiguousarray(
            np.asarray(z["L0_ffn_Wgate"]).astype(np.float32).reshape(8, 128, 8)
            .transpose(1, 0, 2).reshape(128, 64)),
        "l0f_battn_fm": np.zeros((128, 24), np.float32),
        "moe_w1": np.ascontiguousarray(z["L0_ffn_W1"][core]),
        "moe_w2": np.ascontiguousarray(z["L0_ffn_W2"][core]),
        "moe_b1fm": fm_vec(z["L0_ffn_b1"][core]),
        "moe_b2fm": fm_vec(z["L0_ffn_b2"][core]),
        "c_rope_cos": np.ascontiguousarray(cosT),
        "c_rope_sin": np.ascontiguousarray(sinT),
        "c_l0mask": l0m.reshape(4 * 128, 512),
        "c_esel": esel,
        "c_ebase": np.full((128, 1), core * 8192, np.float32),
        "c_oobrow": (3e6 + 100.0 * np.arange(128))[:, None].astype(np.float32),
        "c_oob": np.full((128, 512), 4294967280, np.uint32),
    }
    return m


def build_phaseB_program():
    nc = bass.Bass("TRN2", target_bir_lowering=False)
    io = {n: nc.dram_tensor(n, list(s), dt, kind=k).ap() for n, s, dt, k in PHASEB_DECL}
    declare_phaseB_scratch(nc, io)
    cm = ExitStack()
    with cm:
        prog = Prog(nc, cm)
        prog_init_consts(prog, io)
        emit_phaseB(prog, io)
        if "dma_sync" in prog.sems:
            prog.wait("sync", prog.tok("dma_sync"))
        if "dma_gpsimd" in prog.sems:
            prog.wait("gpsimd", prog.tok("dma_gpsimd"))
        prog.build()
    return nc



def emit_phaseA(prog, io):
    prog.push_scope()
    attnout = sbuf(prog, "a_ao", [128, NCHUNK * T], BF16)
    prog.push_scope()
    h_fm = sbuf(prog, "a_hfm", [128, NCHUNK * T], BF16)

    sc1 = ExitStack()
    prog._scopes.append(sc1)
    st = mk_stats(prog, "a_st")
    xt = [sbuf(prog, f"a_xt{i}", [128, D], F32) for i in range(2)]
    ht = sbuf(prog, "a_ht", [128, D], BF16)
    ps_tr = psum(prog, "a_ptr", [128, 128], BF16)
    xtok = [None, None]
    guard = None
    tr_tok = None
    for ti in range(NTILE):
        b = ti % 2
        if xtok[b] is not None:
            prog.wait("sync", xtok[b])
        dt = prog.dma("sync", xt[b][:, :], io["x_in"][ti * 128:(ti + 1) * 128, :])
        prog.wait("vector", dt)
        prog.wait("scalar", dt)
        if guard is not None:
            prog.wait("vector", guard)
            prog.wait("scalar", guard)
        stok = ln_stats_tm(prog, None, xt[b], st)
        prog.wait("vector", stok)
        t1 = prog.op("vector", lambda e, b=b: e.tensor_scalar(
            ht[:, :], xt[b][:, :], st["negmean"][:, :], st["rstd"][:, :],
            op0=ALU.add, op1=ALU.mult), mark="dve")
        guard = t1
        xtok[b] = t1
        prog.wait("tensor", t1)
        for dc in range(NCHUNK):
            if tr_tok is not None:
                prog.wait("tensor", tr_tok)
            tp = prog.op("tensor", lambda e, dc=dc: e.transpose(
                ps_tr[:, :], ht[:, dc * 128:(dc + 1) * 128], prog.idbf[:, :]), mark="pe")
            prog.wait("vector", tp)
            dst = h_fm[:, dc * T + ti * 128: dc * T + ti * 128 + 128]
            tr_tok = prog.op("vector", lambda e, d=dst: e.tensor_copy(d, ps_tr[:, :]), mark="dve")
    prog.barrier()
    prog._scopes.pop().close()

    sc2 = ExitStack()
    prog._scopes.append(sc2)
    qq = sbuf(prog, "A_qq", [128, T], BF16)
    kk = sbuf(prog, "A_kk", [128, T], BF16)
    vv = sbuf(prog, "A_vv", [128, T], BF16)
    VVa = sbuf(prog, "A_VVa", [128, 32 * 64], BF16)
    VVb = sbuf(prog, "A_VVb", [128, 32 * 64], BF16)
    NPK = [512, 256, 128, 64]
    kpl = [sbuf(prog, f"A_kpl{l}", [128, NPK[l] + 8], BF16) for l in range(4)]
    vpl = [sbuf(prog, f"A_vpl{l}", [128, NPK[l] + 8], BF16) for l in range(4)]
    VPL = [sbuf(prog, f"A_VPL{l}", [128, (NPK[l] // 64) * 64], BF16) for l in range(4)]
    VPLb = [sbuf(prog, f"A_VPLb{l}", [128, (NPK[l] // 64) * 64], BF16) for l in range(4)]
    ptree = sbuf(prog, "A_ptree", [128, 2048], BF16)
    ptree2 = sbuf(prog, "A_ptree2", [128, 1024], BF16)
    P = sbuf(prog, "A_P", [128, 2 * 192], BF16)
    PL = sbuf(prog, "A_PL", [128, 512], BF16)
    PLb = sbuf(prog, "A_PLb", [128, 512], BF16)
    rcp = sbuf(prog, "A_rcp", [1, 1024], F32)
    bc_sb = sbuf(prog, "A_bcsb", [128, 1024], F32)
    band0 = sbuf(prog, "A_band0", [128, 192], BF16)
    band128 = sbuf(prog, "A_band128", [128, 64], BF16)
    lmask = sbuf(prog, "A_lmask", [128, 15 * 512], BF16)
    slcol = sbuf(prog, "A_slcol", [128, 4], BF16)
    wcorr = sbuf(prog, "A_wcorr", [1, 128], BF16)
    bqkv = sbuf(prog, "A_bqkv", [128, 24], F32)
    ps_S = psum(prog, "A_psS", [128, 1024])
    ps_O = psum(prog, "A_psO", [128, 512])
    ps_denA = psum(prog, "A_psdA", [1, 512])
    ps_denB = psum(prog, "A_psdB", [1, 512])
    ps_lvl = psum(prog, "A_pslvl", [128, 512])
    ps_lvlB = psum(prog, "A_pslvlB", [128, 512])
    ps_vt = psum(prog, "A_psvt", [128, 64], BF16)
    ps_p = [ps_S[:, 0:512], ps_S[:, 512:1024]]
    t_ = prog.dma("gpsimd", band0[:, :], io["c_aband0"][:, :])
    t_ = prog.dma("gpsimd", band128[:, :], io["c_aband128"][:, :])
    t_ = prog.dma("gpsimd", lmask[:, :], io["c_almask"][:, :])
    t_ = prog.dma("gpsimd", slcol[:, :], io["c_aslcol"][:, :])
    prog.wait("vector", t_)
    prog.wait("tensor", prog.tok("dma_gpsimd"))
    t_ = prog.dma("gpsimd", wcorr[:, :], io["c_awcorr"][:, :])
    prog.wait("tensor", t_)
    t_ = prog.dma("sync", bqkv[:, :], io["l0a_battn_fm"][:, :])
    prog.wait("scalar", t_)
    prog.wait("tensor", t_)
    qkv_wbufs = [sbuf(prog, f"A_qkvw{i}", [128, NCHUNK * 128], BF16) for i in range(2)]
    h_tok = prog.tok("dve")
    WLV = [64, 128, 256, 512]
    SL = [8, 16, 32, 64]
    norm_guard = None
    vv_guard = None
    exp_tok = None
    expl_tok = None
    pv = None
    for pair in range(8):
        if norm_guard is not None:
            prog.wait("tensor", norm_guard)
        dsts = {"q": (qq, pair), "k": (kk, 8 + pair), "v": (vv, 16 + pair)}
        last_evac = {}
        for nm in ("q", "k", "v"):
            dstbuf, fcol = dsts[nm]

            def writer(fc, tb, pb, mm_tok, tbn_i, dstbuf=dstbuf, fcol=fcol):
                prog.wait("scalar", mm_tok)
                dst = dstbuf[:, tb * 512: tb * 512 + tbn_i]
                return prog.op("scalar", lambda e, d=dst, p=pb, n=tbn_i, fcol=fcol: e.activation(
                    out=d, in_=p[:, 0:n], func=AF.Identity,
                    bias=bqkv[:, fcol:fcol + 1]), mark="act")
            prog.wait("gpsimd", prog.tok("pe"))
            toks = proj_fm(prog, f"A_{nm}", lambda dc, tb, n: h_fm[:, dc * T + tb * 512: dc * T + tb * 512 + n],
                           ("dram", io["l0a_wattn"]), writer, D, 128, T,
                           rhs_tok_fn=lambda _tb: h_tok, ps_banks=ps_p, fstart=fcol,
                           wbufs=qkv_wbufs)
            last_evac[nm] = toks[-1]
        if vv_guard is not None:
            prog.wait("vector", vv_guard)
        ptok = None
        for src_buf, dst_list in ((kk, kpl), (vv, vpl)):
            prog.wait("vector", last_evac["k" if src_buf is kk else "v"])
            t2_ = prog.op("vector", lambda e, s=src_buf: e.tensor_tensor(
                ptree[:, :], s[:, 0:T:2], s[:, 1:T:2], op=ALU.add), mark="dve")
            prog.wait("vector", t2_)
            t4_ = prog.op("vector", lambda e: e.tensor_tensor(
                ptree2[:, :], ptree[:, 0:2048:2], ptree[:, 1:2048:2], op=ALU.add), mark="dve")
            prog.wait("vector", t4_)
            tz_ = prog.op("vector", lambda e, d=dst_list[0]: e.memset(d[:, 0:8], 0.0), mark="dve")
            t8_ = prog.op("vector", lambda e, d=dst_list[0]: e.tensor_tensor(
                d[:, 8:8 + 512], ptree2[:, 0:1024:2], ptree2[:, 1:1024:2], op=ALU.add), mark="dve")
            prog.wait("vector", t8_)
            prev = dst_list[0]
            for l in range(1, 4):
                tz_ = prog.op("vector", lambda e, d=dst_list[l]: e.memset(d[:, 0:8], 0.0), mark="dve")
                tn_ = prog.op("vector", lambda e, d=dst_list[l], p=prev, n=NPK[l]: e.tensor_tensor(
                    d[:, 8:8 + n], p[:, 8:8 + 2 * n:2], p[:, 9:8 + 2 * n:2], op=ALU.add), mark="dve")
                prog.wait("vector", tn_)
                prev = dst_list[l]
            ptok = tn_
        prog.wait("tensor", last_evac["v"])
        vt_tok = None
        for kc in range(32):
            for hh in range(2):
                VVh = VVa if hh == 0 else VVb
                if vt_tok is not None:
                    prog.wait("tensor", vt_tok)
                tp = prog.op("tensor", lambda e, kc=kc, hh=hh: e.transpose(
                    ps_vt[0:128, 0:64], vv[hh * 64:(hh + 1) * 64, kc * 128:(kc + 1) * 128],
                    prog.idbf[hh * 64:(hh + 1) * 64, hh * 64:hh * 64 + 64]), mark="pe")
                prog.wait("vector", tp)
                vt_tok = prog.op("vector", lambda e, kc=kc, VVh=VVh: e.tensor_copy(
                    VVh[:, kc * 64:(kc + 1) * 64], ps_vt[0:128, 0:64]), mark="dve")
        prog.wait("tensor", vt_tok)
        prog.wait("vector", ptok)
        vpl_tok = None
        for l in range(4):
            nch = NPK[l] // 64
            for j in range(nch):
                for hh in range(2):
                    if vpl_tok is not None:
                        prog.wait("tensor", vpl_tok)
                    tp = prog.op("tensor", lambda e, l=l, j=j, hh=hh: e.transpose(
                        ps_vt[0:64, 0:64], vpl[l][hh * 64:(hh + 1) * 64, j * 64:(j + 1) * 64],
                        prog.idbf[hh * 64:(hh + 1) * 64, hh * 64:hh * 64 + 64]), mark="pe")
                    prog.wait("vector", tp)
                    dstv = VPL[l] if hh == 0 else VPLb[l]
                    vpl_tok = prog.op("vector", lambda e, l=l, j=j, dstv=dstv: e.tensor_copy(
                        dstv[0:64, j * 64:(j + 1) * 64], ps_vt[0:64, 0:64]),
                        mark="dve")
        prog.wait("tensor", vpl_tok)
        prog.wait("tensor", ptok)
        prog.wait("tensor", last_evac["q"])
        prog.wait("tensor", last_evac["k"])
        for g in range(8):
            if norm_guard is not None:
                prog.wait("tensor", norm_guard)
            first = True
            import os as _os
            _skip_win = bool(int(_os.environ.get("A_SKIP_WIN", "0")))
            _skip_lvl = bool(int(_os.environ.get("A_SKIP_LVL", "0")))
            kcs = []
            if g > 0:
                kcs.append(4 * g - 1)
            kcs += [4 * g, 4 * g + 1, 4 * g + 2, 4 * g + 3]
            for kc in ([] if _skip_win else kcs):
                qlo = max(512 * g, 128 * kc)
                qhi = min(512 * g + 512, 128 * kc + 192, T)
                w = qhi - qlo
                if w <= 0:
                    continue
                tail = qlo > 128 * kc
                if exp_tok is not None:
                    prog.wait("tensor", exp_tok)
                smm = None
                for hh in range(2):
                    smm = prog.op("tensor", lambda e, kc=kc, hh=hh, qlo=qlo, w=w: e.matmul(
                        ps_S[:, hh * 512: hh * 512 + w],
                        kk[hh * 64:(hh + 1) * 64, kc * 128:(kc + 1) * 128],
                        qq[hh * 64:(hh + 1) * 64, qlo:qlo + w],
                        start=True, stop=True), mark="pe" if hh == 1 else None)
                prog.wait("scalar", smm)
                if pv is not None:
                    prog.wait("scalar", pv)
                prog.op("scalar", lambda e, w=w: e.activation(
                    out=P[:, 0:192], in_=ps_S[:, 0:192], func=AF.Exp, scale=0.125))
                exp_tok = prog.op("scalar", lambda e, w=w: e.activation(
                    out=P[:, 192:384], in_=ps_S[:, 512:704], func=AF.Exp, scale=0.125), mark="act")
                prog.wait("vector", exp_tok)
                mtoks = []
                for hh in range(2):
                    mb = band128 if tail else band0
                    mt_ = prog.op("vector", lambda e, hh=hh, w=w, mb=mb: e.tensor_tensor(
                        P[:, hh * 192: hh * 192 + w], P[:, hh * 192: hh * 192 + w],
                        mb[:, 0:w], op=ALU.mult), mark="dve")
                    mtoks.append(mt_)
                prog.wait("tensor", mtoks[-1])
                qoff = qlo - 512 * g
                if bool(int(_os.environ.get("A_SKIP_PV", "0"))):
                    pv = prog.op("tensor", lambda e, f=first, qoff=qoff, w=w: e.matmul(
                        ps_O[0:64, qoff:qoff + w], VVa[:, kc * 64:(kc + 1) * 64], P[:, 0:w],
                        start=f, stop=False, tile_position=(0, 0)), mark="pe")
                    first = False
                    continue
                prog.op("tensor", lambda e, f=first, kc=kc, qoff=qoff, w=w: e.matmul(
                    ps_O[0:64, qoff:qoff + w], VVa[:, kc * 64:(kc + 1) * 64], P[:, 0:w],
                    start=f, stop=False, tile_position=(0, 0)))
                prog.op("tensor", lambda e, f=first, kc=kc, qoff=qoff, w=w: e.matmul(
                    ps_O[64:128, qoff:qoff + w], VVb[:, kc * 64:(kc + 1) * 64], P[:, 192:192 + w],
                    start=f, stop=False, tile_position=(0, 64)))
                prog.op("tensor", lambda e, f=first, qoff=qoff, w=w: e.matmul(
                    ps_denA[0:1, qoff:qoff + w], prog.ones_bf[:, :], P[:, 0:w],
                    start=f, stop=False))
                pv = prog.op("tensor", lambda e, f=first, qoff=qoff, w=w, st_=(_skip_lvl and kc == kcs[-1]): e.matmul(
                    ps_denB[0:1, qoff:qoff + w], prog.ones_bf[:, :], P[:, 192:192 + w],
                    start=f, stop=st_), mark="pe")
                first = False
            if g == 0 and not _skip_win and not bool(int(_os.environ.get("A_SKIP_CORR", "0"))):
                prog.op("tensor", lambda e: e.matmul(
                    ps_denA[0:1, 0:128], prog.ones_bf[0:1, 0:1], wcorr[0:1, :],
                    start=False, stop=False))
                pv = prog.op("tensor", lambda e: e.matmul(
                    ps_denB[0:1, 0:128], prog.ones_bf[0:1, 0:1], wcorr[0:1, :],
                    start=False, stop=False), mark="pe")
            for l in ([] if _skip_lvl else list(range(4))):
                clen = 8 * WLV[l]
                j = (512 * g) // clen
                pk0 = 64 * j
                sub = (512 * g - j * clen) // 512
                if expl_tok is not None:
                    prog.wait("tensor", expl_tok)
                am = None
                for hh in range(2):
                    psl = ps_lvl if hh == 0 else ps_lvlB
                    am = prog.op("tensor", lambda e, l=l, hh=hh, pk0=pk0, g=g, psl=psl: e.matmul(
                        psl[0:64, :],
                        kpl[l][hh * 64:(hh + 1) * 64, pk0: pk0 + 64],
                        qq[hh * 64:(hh + 1) * 64, g * 512:(g + 1) * 512],
                        start=True, stop=True, tile_position=(0, 0) if hh == 0 else (64, 0)),
                        mark="pe" if hh == 1 else None)
                prog.wait("scalar", am)
                if pv is not None:
                    prog.wait("scalar", pv)
                prog.op("scalar", lambda e, l=l: e.activation(
                    out=PL[0:64, :], in_=ps_lvl[0:64, :], func=AF.Exp,
                    scale=float(np.float32(1.0 / (8 * SL[l])))))
                expl_tok = prog.op("scalar", lambda e, l=l: e.activation(
                    out=PLb[0:64, :], in_=ps_lvlB[0:64, :], func=AF.Exp,
                    scale=float(np.float32(1.0 / (8 * SL[l])))), mark="act")
                prog.wait("vector", expl_tok)
                mvar = [0, 1, 3, 7][l] + sub
                mtk = []
                for hh in range(2):
                    PLh = PL if hh == 0 else PLb
                    mt_ = prog.op("vector", lambda e, PLh=PLh, mvar=mvar: e.tensor_tensor(
                        PLh[0:64, :], PLh[0:64, :],
                        lmask[0:64, mvar * 512:(mvar + 1) * 512], op=ALU.mult), mark="dve")
                    mtk.append(mt_)
                prog.wait("tensor", mtk[-1])
                prog.op("tensor", lambda e, l=l, pk0=pk0, f=(_skip_win and l == 0): e.matmul(
                    ps_O[0:64, :], VPL[l][0:64, pk0:pk0 + 64], PL[0:64, :],
                    start=f, stop=(l == 3), tile_position=(0, 0)))
                prog.op("tensor", lambda e, l=l, pk0=pk0, f=(_skip_win and l == 0): e.matmul(
                    ps_O[64:128, :], VPLb[l][0:64, pk0:pk0 + 64], PLb[0:64, :],
                    start=f, stop=(l == 3), tile_position=(0, 64)))
                prog.op("tensor", lambda e, l=l, f=(_skip_win and l == 0): e.matmul(
                    ps_denA[0:1, :], slcol[0:64, l:l + 1], PL[0:64, :],
                    start=f, stop=(l == 3), tile_position=(0, 0)))
                pv = prog.op("tensor", lambda e, l=l, f=(_skip_win and l == 0): e.matmul(
                    ps_denB[0:1, :], slcol[0:64, l:l + 1], PLb[0:64, :],
                    start=f, stop=(l == 3), tile_position=(0, 0)), mark="pe")
            if norm_guard is not None:
                prog.wait("vector", norm_guard)
            prog.wait("vector", pv)
            tr1 = prog.op("vector", lambda e: e.reciprocal(rcp[:, 0:512], ps_denA[:, :]), mark="dve")
            tr1 = prog.op("vector", lambda e: e.reciprocal(rcp[:, 512:1024], ps_denB[:, :]), mark="dve")
            prog.wait("vector", tr1)
            prog.wait("tensor", tr1)
            tb1 = None
            for c0 in (0, 512):
                tm_ = prog.op("tensor", lambda e, c0=c0: e.matmul(
                    ps_lvl[:, 0:512], prog.ones_f32[:, :], rcp[0:1, c0:c0 + 512],
                    start=True, stop=True), mark="pe")
                prog.wait("vector", tm_)
                tb1 = prog.op("vector", lambda e, c0=c0: e.tensor_copy(
                    bc_sb[:, c0:c0 + 512], ps_lvl[:, 0:512]), mark="dve")
                prog.wait("tensor", tb1)
            prog.wait("vector", tb1)
            toks = []
            for hh in range(2):
                dst = attnout[hh * 64:(hh + 1) * 64, pair * T + g * 512: pair * T + (g + 1) * 512]
                tno = prog.op("vector", lambda e, hh=hh, d=dst: e.tensor_tensor(
                    d, ps_O[hh * 64:(hh + 1) * 64, :],
                    bc_sb[hh * 64:(hh + 1) * 64, hh * 512:(hh + 1) * 512], op=ALU.mult),
                    mark="dve")
                toks.append(tno)
            norm_guard = toks[-1]
        vv_guard = norm_guard
    prog.barrier()
    prog._scopes.pop().close()
    prog.pop_scope()

    sc3 = ExitStack()
    prog._scopes.append(sc3)
    yo_fm = sbuf(prog, "a3_yo", [128, NCHUNK * T], BF16)
    bos = sbuf(prog, "a3_bo", [128, 8], F32)
    tb_ = prog.dma("sync", bos[:, :], io["l0a_bproj_fm"][:, :])
    prog.wait("scalar", tb_)
    ao_tok = prog.tok("dve")
    otoks = proj_fm(prog, "a3_o", lambda dc, tb, n: attnout[:, dc * T + tb * 512: tb * 512 + dc * T + n],
                    ("dram", io["l0a_wproj"]),
                    act_evac_writer(prog, yo_fm, T, bias_sb=bos, act=AF.Copy),
                    D, D, T, rhs_tok_fn=lambda _tb: ao_tok)
    ylast = otoks[-1]
    st3 = mk_stats(prog, "a3_st")
    xt3 = [sbuf(prog, f"a3_xt{i}", [128, D], F32) for i in range(2)]
    ot3 = [sbuf(prog, f"a3_ot{i}", [128, D], F32) for i in range(2)]
    yt3 = sbuf(prog, "a3_yt", [128, D], F32)
    ps_trf = psum(prog, "a3_ptrf", [128, 128], BF16)
    xtok = [None, None]
    otok = [None, None]
    trf_tok = None
    guard = None
    prog.wait("tensor", ylast)
    for ti in range(NTILE):
        b = ti % 2
        for dc in range(NCHUNK):
            if trf_tok is not None:
                prog.wait("tensor", trf_tok)
            tp = prog.op("tensor", lambda e, dc=dc, ti=ti: e.transpose(
                ps_trf[:, :], yo_fm[:, dc * T + ti * 128: dc * T + ti * 128 + 128],
                prog.idbf[:, :]), mark="pe")
            prog.wait("vector", tp)
            trf_tok = prog.op("vector", lambda e, dc=dc: e.tensor_copy(
                yt3[:, dc * 128:(dc + 1) * 128], ps_trf[:, :]), mark="dve")
        prog.wait("vector", trf_tok)
        prog.wait("scalar", trf_tok)
        if guard is not None:
            prog.wait("vector", guard)
            prog.wait("scalar", guard)
        stok = ln_stats_tm(prog, None, yt3, st3)
        if xtok[b] is not None:
            prog.wait("sync", xtok[b])
        dt = prog.dma("sync", xt3[b][:, :], io["x_in"][ti * 128:(ti + 1) * 128, :])
        prog.wait("vector", stok)
        tn = prog.op("vector", lambda e: e.tensor_scalar(
            yt3[:, :], yt3[:, :], st3["negmean"][:, :], st3["rstd"][:, :],
            op0=ALU.add, op1=ALU.mult), mark="dve")
        guard = tn
        prog.wait("vector", dt)
        prog.wait("vector", tn)
        if otok[b] is not None:
            prog.wait("vector", otok[b])
        t3_ = prog.op("vector", lambda e, b=b: e.tensor_tensor(
            ot3[b][:, :], xt3[b][:, :], yt3[:, :], op=ALU.add), mark="dve")
        guard = t3_
        xtok[b] = t3_
        prog.wait("sync", t3_)
        otok[b] = prog.dma("sync", io["x_out"][ti * 128:(ti + 1) * 128, :], ot3[b][:, :])
    prog._scopes.pop().close()
    prog.pop_scope()


PHASEA_DECL = [
    ("x_in", (T, D), F32, "ExternalInput"),
    ("l0a_wattn", (D, 3 * D), F32, "ExternalInput"),
    ("l0a_wproj", (D, D), F32, "ExternalInput"),
    ("l0a_battn_fm", (128, 24), F32, "ExternalInput"),
    ("l0a_bproj_fm", (128, 8), F32, "ExternalInput"),
    ("c_aband0", (128, 192), F32, "ExternalInput"),
    ("c_aband128", (128, 64), F32, "ExternalInput"),
    ("c_almask", (128, 15 * 512), F32, "ExternalInput"),
    ("c_aslcol", (128, 4), F32, "ExternalInput"),
    ("c_awcorr", (1, 128), F32, "ExternalInput"),
    ("c_ident_bf", (128, 128), BF16, "ExternalInput"),
    ("c_ident_f32", (128, 128), F32, "ExternalInput"),
    ("c_ones", (1, 128), F32, "ExternalInput"),
    ("c_ones_col", (128, 1), BF16, "ExternalInput"),
    ("x_out", (T, D), F32, "ExternalOutput"),
]


def host_phaseA_inputs(z, consts):
    k = np.arange(128)[:, None]
    j = np.arange(192)[None, :]
    band0 = ((j - k >= 0) & (j - k <= 63)).astype(np.float32)
    j2 = np.arange(64)[None, :]
    band128 = ((128 + j2 - k >= 0) & (128 + j2 - k <= 63)).astype(np.float32)
    lm = np.zeros((64, 15 * 512), np.float32)
    pk = np.arange(64)[:, None]
    qq_ = np.arange(512)[None, :]
    base = [0, 1, 3, 7]
    for l in range(4):
        w = 64 * (2 ** l)
        for sub in range(2 ** l):
            blk = (sub * 512 + qq_) // w
            lm[:, (base[l] + sub) * 512:(base[l] + sub + 1) * 512] = (pk // 8 == blk)
    slc = np.zeros((128, 4), np.float32)
    for l in range(4):
        slc[:, l] = float(8 * (2 ** l))
    wc = np.maximum(0, 63 - np.arange(128))[None, :].astype(np.float32)
    m = {
        "l0a_wattn": np.ascontiguousarray(z["L0_attn_W_attn"]),
        "l0a_wproj": np.ascontiguousarray(z["L0_attn_W_proj"]),
        "l0a_battn_fm": fm_vec(z["L0_attn_b_attn"]),
        "l0a_bproj_fm": fm_vec(z["L0_attn_b_proj"]),
        "c_aband0": band0,
        "c_aband128": band128,
        "c_almask": np.concatenate([lm, lm], 0),
        "c_aslcol": slc,
        "c_awcorr": wc,
    }
    return m


def build_phaseA_program():
    nc = bass.Bass("TRN2", target_bir_lowering=False)
    io = {n: nc.dram_tensor(n, list(s), dt, kind=k).ap() for n, s, dt, k in PHASEA_DECL}
    cm = ExitStack()
    with cm:
        prog = Prog(nc, cm)
        prog_init_consts(prog, io)
        emit_phaseA(prog, io)
        if "dma_sync" in prog.sems:
            prog.wait("sync", prog.tok("dma_sync"))
        if "dma_gpsimd" in prog.sems:
            prog.wait("gpsimd", prog.tok("dma_gpsimd"))
        prog.build()
    return nc



_PROG_CACHE = {}


def _build_cached(name, fn):
    if name not in _PROG_CACHE:
        _PROG_CACHE[name] = fn()
    return _PROG_CACHE[name]


def build_phaseB_program_():
    return build_phaseB_program()


def kernel(**inputs):
    z = inputs
    consts = host_consts()
    x = np.ascontiguousarray(np.asarray(z["L0_attn_x"], dtype=np.float32))

    def run_phase(nc, base_maps, xcur, outkey="x_out"):
        in_maps = []
        for c in range(NCORE):
            m = dict(base_maps[c])
            m["x_in"] = np.ascontiguousarray(xcur[c])
            in_maps.append(m)
        res = run_bass_kernel_spmd(nc, in_maps, core_ids=list(range(NCORE)))
        return np.stack([res.results[c][outkey] for c in range(NCORE)])

    ckeys = ("c_ident_bf", "c_ident_f32", "c_ones", "c_ones_col")

    ncA = _build_cached("A", build_phaseA_program)
    baseA = host_phaseA_inputs(z, consts)
    for k in ckeys:
        baseA[k] = consts[k]
    x = run_phase(ncA, [baseA] * NCORE, x)

    ncB = _build_cached("B", build_phaseB_program)
    basesB = []
    for c in range(NCORE):
        m = host_phaseB_inputs(z, consts, c)
        for k in ckeys:
            m[k] = consts[k]
        basesB.append(m)
    x = run_phase(ncB, basesB, x)

    ncC = _build_cached("C", build_phaseC_program)
    baseC = host_phaseC_inputs(z, consts)
    for k in ckeys:
        baseC[k] = consts[k]
    x = run_phase(ncC, [baseC] * NCORE, x)

    ncD = _build_cached("D", build_phaseD_program)
    baseD = {
        "l1f_w1": np.ascontiguousarray(np.asarray(z["L1_ffn_W1"], dtype=np.float32)),
        "l1f_w2": np.ascontiguousarray(np.asarray(z["L1_ffn_W2"], dtype=np.float32)),
        "l1f_b1fm": fm_vec(z["L1_ffn_b1"]),
        "l1f_b2fm": fm_vec(z["L1_ffn_b2"]),
    }
    for k in ckeys:
        baseD[k] = consts[k]
    x = run_phase(ncD, [baseD] * NCORE, x)
    return x.astype(np.float32)
